# version 06
# baseline (speedup 1.0000x reference)
from contextlib import ExitStack

import numpy as np

import concourse.bass as bass
import concourse.mybir as mybir
from concourse.bass_utils import run_bass_kernel_spmd

F32 = mybir.dt.float32
BF16 = mybir.dt.bfloat16
AF = mybir.ActivationFunctionType
OP = mybir.AluOpType
AX = mybir.AxisListType

NCORES = 8
S = 8192
T = S // NCORES
E = 2048
HD = 128
NH = 16
NKV = 8
FCH = E // 128
TCH = T // 128
KCH = S // 128
RANK = 16
NEXP = 16
ER = RANK * NEXP
RHID = 64
INV_SQRT_E = float(np.float32(1.0 / np.sqrt(np.float64(E))))
LN_EPS = 1e-5

KW = NKV * 128 * (T // 2)
VW = NKV * TCH * 128 * (HD // 2)
PAY = 2 * KW + 2 * VW + NKV

STAGE = 4


class Builder:
    def __init__(self, stage=STAGE, debug=()):
        self.stage = stage
        self.debug = list(debug)
        self.nc = bass.Bass("TRN2", target_bir_lowering=False, debug=False)
        self.stack = ExitStack()
        self.ops = []
        self.sems = {}
        self.dbg = {}

    def sem(self, name):
        h = self.stack.enter_context(self.nc.semaphore(name))
        self.sems[name] = [h, 0]

    def sb(self, name, shape, dt=F32):
        return self.stack.enter_context(self.nc.sbuf_tensor(name, shape, dt))

    def ps(self, name, shape):
        return self.stack.enter_context(self.nc.psum_tensor(name, shape, F32))

    def on(self, eng, fn):
        self.ops.append((eng, fn))

    def wait(self, eng, name, value=None):
        h = self.sems[name][0]
        v = self.sems[name][1] if value is None else value
        if v > 0:
            self.on(eng, lambda e, h=h, v=v: e.wait_ge(h, v))

    def wait_tok(self, eng, tok):
        if tok is not None:
            self.wait(eng, tok[0], tok[1])

    DRAIN_ENGS = ("vector", "scalar", "gpsimd")

    def op(self, eng, fn, sem=None, n=1, nd=False):
        if eng in self.DRAIN_ENGS and not nd:
            self.on(eng, lambda e: e.drain())
        if sem is None:
            self.on(eng, fn)
            return None
        rec = self.sems[sem]
        rec[1] += n
        inc = (rec[0], n)
        self.on(eng, lambda e, fn=fn, inc=inc: fn(e).then_inc(*inc))
        return (sem, rec[1])

    def dma(self, eng, out, in_, sem):
        return self.op(eng, lambda e, out=out, in_=in_: e.dma_start(out=out, in_=in_),
                       sem=sem, n=16)

    def mm3(self, out, ah, al, bh, bl, start, stop, sem=None):
        self.on("tensor", lambda t, o=out, a=ah, b_=bh, s=start: t.matmul(o, a, b_, start=s, stop=False))
        self.on("tensor", lambda t, o=out, a=ah, b_=bl: t.matmul(o, a, b_, start=False, stop=False))
        return self.op("tensor", lambda t, o=out, a=al, b_=bh, p=stop: t.matmul(o, a, b_, start=False, stop=p), sem=sem)

    def dump(self, name, ap):
        if name not in self.debug:
            return
        d = self.nc.dram_tensor("dbg_" + name, list(ap.shape), ap.dtype,
                                kind="ExternalOutput").ap()
        self.dbg[name] = d
        for e_ in self.sems:
            if e_ not in ("dout",):
                self.wait("sync", e_)
        self.dma("sync", d, ap, "dout")


def build(stage=STAGE, debug=()):
    b = Builder(stage, debug)
    nc = b.nc

    din = lambda n, s: nc.dram_tensor(n, s, F32, kind="ExternalInput").ap()
    x_in = din("x", [T, E])
    wq = din("wq", [NH, 2, 8, 128, 128])
    wk = din("wk", [NKV, 2, 8, 128, 128])
    wv = din("wv", [NKV, 2, 8, 128, 128])
    wo = din("wo", [4, FCH, 128, 512])
    wff = din("wff", [4, FCH, 128, 512])
    wa = din("wa", [2, 2, 8, 128, 128])
    wb = din("wb", [4, 2, 128, 512])
    rw1 = din("rw1", [FCH, 128, RHID])
    rw2 = din("rw2", [RHID, 32])
    ffb = din("ffb", [4, 512])
    rb1 = din("rb1", [RHID, 1])
    rbg = din("rbg", [NEXP, 1])
    eye = din("eye", [128, 128])
    cosq = din("cosq", [128, T])
    sinm = din("sinm", [128, T])
    emat = din("emat", [16, ER])
    y_out = nc.dram_tensor("y", [T, E], F32, kind="ExternalOutput").ap()
    pay_in = nc.dram_tensor("pay_in", [PAY], F32).ap()
    pay_out = nc.dram_tensor("pay_out", [NCORES, PAY], F32, addr_space="Shared").ap()

    kpay = pay_in[0:2 * KW].rearrange("(s j p w) -> s j p w", s=2, j=NKV, p=128)
    vpay = pay_in[2 * KW:2 * KW + 2 * VW].rearrange("(s j c p w) -> s j c p w",
                                                    s=2, j=NKV, c=TCH, p=128)
    npay = pay_in[2 * KW + 2 * VW: 2 * KW + 2 * VW + NKV]

    A = b.sb("arenaA", [128, 16384])
    B = b.sb("arenaB", [128, 16384])
    X = b.sb("arenaX", [128, 2048])
    Abf = A[:].bitcast(BF16)
    Bbf = B[:].bitcast(BF16)

    xt = A[:, 0:2048]
    lnt = X[:, 0:2048]
    QC = A
    def lnh_of(abf, fc, t0, t1):
        return abf[:, fc * 2048 + t0: fc * 2048 + t1]
    def lnl_of(abf, fc, t0, t1):
        return abf[:, fc * 2048 + 1024 + t0: fc * 2048 + 1024 + t1]

    pbuf = B[:, 0:2048]
    ph_ = Bbf[:, 4096:8192]
    pl_ = Bbf[:, 6144:8192]
    phv = Bbf[:, 4096:6144]
    kvb0 = Bbf[:, 8192:12288]
    kvb1 = Bbf[:, 12288:16384]
    qh_s = Bbf[:, 16384:17408]
    ql_s = Bbf[:, 17408:18432]
    inv_rs = B[:, 9216:10240]
    ctmp = B[:, 10240:11264]
    qsq = B[:, 11264:12288]
    x2 = B

    eyet = b.sb("eyet", [128, 128])
    cosqt = b.sb("cosqt", [128, T])
    sinmt = b.sb("sinmt", [128, T])
    onesf = b.sb("onesf", [128, 128])
    onesb = b.sb("onesb", [128, 64], BF16)
    ematt = b.sb("ematt", [16, ER])
    rb1t = b.sb("rb1t", [RHID, 1])
    rbgt = b.sb("rbgt", [NEXP, 1])
    fbias = b.sb("fbias", [1, 512])
    st1 = b.sb("st1", [128, 8])
    sc1 = b.sb("sc1", [128, 32])
    sc2 = b.sb("sc2", [128, 32])
    knorm2 = b.sb("knorm2", [1, NKV])
    kn_all = b.sb("kn_all", [1, NKV * NCORES])
    kmax2 = b.sb("kmax2", [1, NKV])
    qmax2 = b.sb("qmax2", [1, NH])
    mh = b.sb("mh", [128, NH])
    ev = b.sb("ev", [128, 1024])
    kvstage = b.sb("kvstage", [128, 1024], BF16)
    wstage = b.sb("wstage", [128, 3072])
    wbh = b.sb("wbh", [128, 3072], BF16)
    wbl = b.sb("wbl", [128, 3072], BF16)
    h_fm = b.sb("h_fm", [RHID, T])
    h_h = b.sb("h_h", [RHID, T], BF16)
    h_l = b.sb("h_l", [RHID, T], BF16)
    r2h = b.sb("r2h", [RHID, 32], BF16)
    r2l = b.sb("r2l", [RHID, 32], BF16)
    lg_fm = b.sb("lg_fm", [16, T])
    lg_tm = b.sb("lg_tm", [128, 16])
    msk1 = b.sb("msk1", [128, 16])
    msk2 = b.sb("msk2", [128, 16])
    ind1 = b.sb("ind1", [128, 16])
    cf_tm = b.sb("cf_tm", [128, 16])
    cf_fm = b.sb("cf_fm", [16, T])
    wxh = b.sb("wxh", [128, 2 * T], BF16)
    wxl = b.sb("wxl", [128, 2 * T], BF16)
    outt = b.sb("outt", [128, 1024])
    cfe = X[:, 0:1024]
    xa_t = X[:, 1024:2048]

    psS0 = b.ps("psS0", [128, 1024])
    psS1 = b.ps("psS1", [128, 1024])
    psC = b.ps("psC", [128, 1024])
    psR = b.ps("psR", [128, 512])
    psX = b.ps("psX", [128, 512])
    bank = [psS0[:, 0:512], psS0[:, 512:1024], psS1[:, 0:512], psS1[:, 512:1024],
            psC[:, 0:512], psC[:, 512:1024], psR[:, :], psX[:, :]]

    for s in ["d", "dw0", "dw1", "dw2", "dkv0", "dkv1", "dout", "dout0", "dout1", "vr",
              "dn", "dfb", "tt", "tq", "ts", "tp", "va", "vh", "vb", "sa", "se", "sh", "ga", "gl", "cc"]:
        b.sem(s)

    for dst, src in [(eyet[:], eye), (cosqt[:], cosq), (sinmt[:], sinm),
                     (ematt[:], emat), (rb1t[:], rb1), (rbgt[:], rbg)]:
        b.dma("sync", dst, src, "d")
    b.wait("vector", "d")
    b.op("vector", lambda v: v.memset(onesf[:], 1.0))
    b.op("vector", lambda v: v.tensor_copy(onesb[:], onesf[:, 0:64]), sem="va")

    def emit_ln(src_tile):
        b.wait("scalar", "tt")
        b.op("vector", lambda v, s_=src_tile: v.reduce_sum(st1[:, 0:1], s_, axis=AX.X))
        t_nm = b.op("vector", lambda v: v.tensor_scalar(
            out=st1[:, 1:2], in0=st1[:, 0:1], scalar1=-1.0 / E, scalar2=None,
            op0=OP.mult), sem="va")
        b.wait_tok("scalar", t_nm)
        t_sq = b.op("scalar", lambda s, s_=src_tile: s.activation(
            out=lnt, in_=s_, func=AF.Square, bias=st1[:, 1:2], scale=1.0,
            accum_out=st1[:, 2:3]), sem="sa")
        b.wait_tok("vector", t_sq)
        b.op("vector", lambda v: v.tensor_scalar(
            out=st1[:, 3:4], in0=st1[:, 2:3], scalar1=1.0 / E, scalar2=LN_EPS,
            op0=OP.mult, op1=OP.add), sem="va")
        b.wait("scalar", "va")
        t_rt = b.op("scalar", lambda s: s.activation(
            out=st1[:, 4:5], in_=st1[:, 3:4], func=AF.Sqrt), sem="sa")
        b.wait_tok("vector", t_rt)
        b.op("vector", lambda v: v.reciprocal(st1[:, 5:6], st1[:, 4:5]))
        b.op("vector", lambda v: v.tensor_tensor(out=st1[:, 6:7], in0=st1[:, 5:6], in1=st1[:, 5:6], op=OP.mult))
        b.op("vector", lambda v: v.tensor_tensor(out=st1[:, 6:7], in0=st1[:, 6:7], in1=st1[:, 3:4], op=OP.mult))
        b.op("vector", lambda v: v.tensor_scalar(out=st1[:, 6:7], in0=st1[:, 6:7], scalar1=-0.5, scalar2=1.5, op0=OP.mult, op1=OP.add))
        b.op("vector", lambda v: v.tensor_tensor(out=st1[:, 7:8], in0=st1[:, 5:6], in1=st1[:, 6:7], op=OP.mult))
        return b.op("vector", lambda v, s_=src_tile: v.tensor_scalar(
            out=lnt, in0=s_, scalar1=st1[:, 1:2], scalar2=st1[:, 7:8],
            op0=OP.add, op1=OP.mult), sem="va")

    def emit_transpose_splits(tc, dst_bf, apply_tok):
        toks = [None] * FCH
        for fc in range(FCH):
            slot = fc % 2
            pst = psS0[:, slot * 512: slot * 512 + 128]
            b.wait_tok("tensor", apply_tok)
            if fc >= 2:
                b.wait_tok("tensor", toks[fc - 2])
            t_tr = b.op("tensor", lambda t, fc=fc, pst=pst: t.transpose(
                pst, lnt[:, fc * 128:(fc + 1) * 128], eyet[:]), sem="tt")
            hs = lnh_of(dst_bf, fc, tc * 128, tc * 128 + 128)
            ls = lnl_of(dst_bf, fc, tc * 128, tc * 128 + 128)
            b.wait_tok("scalar", t_tr)
            t_hi = b.op("scalar", lambda s, pst=pst, hs=hs: s.activation(
                out=hs, in_=pst, func=AF.Copy), sem="sa", nd=True)
            b.wait_tok("vector", t_hi)
            toks[fc] = b.op("vector", lambda v, pst=pst, hs=hs, ls=ls: v.tensor_tensor(
                out=ls, in0=pst, in1=hs, op=OP.subtract), sem="vb", nd=True)
        return toks[-1]

    for tc in range(TCH):
        b.wait("sync", "va")
        t_x = b.dma("sync", xt, x_in.rearrange("(c p) e -> c p e", p=128)[tc], "d")
        b.wait_tok("vector", t_x)
        b.wait_tok("scalar", t_x)
        t_ap = emit_ln(xt)
        emit_transpose_splits(tc, Bbf, t_ap)

    b.dump("ln1b", B[:])

    if stage < 1:
        t_z = b.op("vector", lambda v: v.memset(ev[:, 0:512], 0.0), sem="va")
        b.wait_tok("sync", t_z)
        for tc in range(TCH):
            for oh in range(4):
                b.dma("sync", y_out[tc * 128:(tc + 1) * 128, oh * 512:(oh + 1) * 512],
                      ev[:, 0:512], "dout")
        b.wait("sync", "dout")
        _replay(b)
        return b

    b._kv_tok = None

    wslot = {"tok": [None, None, None], "split": [None, None, None]}

    def weight_half(dram_ap, ncols, slot, free_tok=-1, split_toks=-1, lo_eng="gpsimd"):
        sl = slice(slot * 1024, slot * 1024 + ncols)
        b.wait_tok("sync", wslot["tok"][slot])
        if wslot["split"][slot] is not None:
            b.wait_tok("sync", wslot["split"][slot][0])
            b.wait_tok("sync", wslot["split"][slot][1])
        dst = wstage[:, sl]
        if dram_ap.ndim == 3:
            dst = dst.rearrange("p (f o) -> p f o", f=dram_ap.shape[1])
        t_d = b.dma("sync", dst, dram_ap, "dw%d" % slot)
        b.wait_tok("vector", t_d)
        b.wait_tok("vector", wslot["tok"][slot])
        b.wait_tok("gpsimd", t_d)
        b.wait_tok("gpsimd", wslot["tok"][slot])
        hs = wbh[:, sl]
        ls = wbl[:, sl]
        t_v = b.op("vector", lambda v, sl=sl, hs=hs: v.tensor_copy(hs, wstage[:, sl]), sem="va", nd=True)
        if lo_eng == "gpsimd":
            b.wait_tok("gpsimd", t_v)
            t_g = b.op("gpsimd", lambda g, sl=sl, hs=hs, ls=ls: g.tensor_tensor(
                out=ls, in0=wstage[:, sl], in1=hs, op=OP.subtract), sem="ga", nd=True)
        else:
            t_g = b.op("vector", lambda v, sl=sl, hs=hs, ls=ls: v.tensor_tensor(
                out=ls, in0=wstage[:, sl], in1=hs, op=OP.subtract), sem="va")
        wslot["split"][slot] = (t_v, t_g)
        return hs, ls, t_v, t_g

    def wslot_consumed(slot, tok):
        wslot["tok"][slot] = tok

    def rope_pair(pst, dst, tsl, sem):
        b.op("vector", lambda v, pst=pst, tsl=tsl: v.tensor_tensor(
            out=ev[:, 0:512], in0=pst, in1=cosqt[:, tsl], op=OP.mult))
        b.op("vector", lambda v, pst=pst, dst=dst, tsl=tsl: v.tensor_tensor(
            out=dst[0:64, :], in0=pst[64:128, :], in1=sinmt[0:64, tsl], op=OP.mult))
        b.op("vector", lambda v, pst=pst, dst=dst, tsl=tsl: v.tensor_tensor(
            out=dst[64:128, :], in0=pst[0:64, :], in1=sinmt[64:128, tsl], op=OP.mult))
        return b.op("vector", lambda v, dst=dst: v.tensor_tensor(
            out=dst, in0=dst, in1=ev[:, 0:512], op=OP.add), sem=sem)

    qkv_state = {"ps": [None, None, None, None]}
    qkv_banks = [psS1[:, 0:512], psS1[:, 512:1024], psS0[:, 0:512], psS0[:, 512:1024]]

    def qkv(w_ap, n_oc, evac, lo_eng="gpsimd"):
        prev_ps_tok = qkv_state["ps"]
        nhalf = 2 * n_oc
        fetched = {}

        def fetch(i):
            if i >= nhalf or i in fetched:
                return
            oc, hf = divmod(i, 2)
            fetched[i] = (weight_half(
                w_ap[oc, hf].rearrange("f p o -> p f o"), 1024, i % 3,
                lo_eng=lo_eng), i % 3)

        for i in range(3):
            fetch(i)
        for oc in range(n_oc):
            halves = [fetched[2 * oc], fetched[2 * oc + 1]]
            for th in range(2):
                pslot = (oc * 2 + th) % 4
                pst = qkv_banks[pslot]
                b.wait_tok("tensor", prev_ps_tok[pslot])
                t_mm = None
                for fc in range(FCH):
                    hf, fcl = divmod(fc, 8)
                    (hs, ls, t_v, t_g), slot = halves[hf]
                    if fcl == 0:
                        b.wait_tok("tensor", t_v)
                        b.wait_tok("tensor", t_g)
                    wh_ = hs[:, fcl * 128:(fcl + 1) * 128]
                    wl_ = ls[:, fcl * 128:(fcl + 1) * 128]
                    rh_ = lnh_of(Bbf, fc, th * 512, th * 512 + 512)
                    rl_ = lnl_of(Bbf, fc, th * 512, th * 512 + 512)
                    t_mm = b.mm3(pst, wh_, wl_, rh_, rl_, start=(fc == 0),
                                 stop=(fc == FCH - 1), sem="tq" if fc == FCH - 1 else None)
                prev_ps_tok[pslot] = evac(oc, th, pst, t_mm)
            wslot_consumed(fetched[2 * oc][1], t_mm)
            wslot_consumed(fetched[2 * oc + 1][1], t_mm)
            fetch(2 * oc + 3)
            fetch(2 * oc + 4)

    def evac_q(oc, th, pst, t_mm):
        tsl = slice(th * 512, (th + 1) * 512)
        dst = QC[:, oc * 1024 + th * 512: oc * 1024 + (th + 1) * 512]
        b.wait_tok("vector", t_mm)
        return rope_pair(pst, dst, tsl, "va")

    def evac_k(oc, th, pst, t_mm):
        j = oc
        tsl = slice(th * 512, (th + 1) * 512)
        f_ = ev[:, 512:1024]
        b.wait_tok("vector", t_mm)
        t_r = rope_pair(pst, f_, tsl, "va")
        kh = kvstage[:, 0:512]
        kl = kvstage[:, 512:1024]
        b.wait_tok("scalar", t_r)
        b.wait_tok("scalar", b._kv_tok)
        t_h = b.op("scalar", lambda s, kh=kh, f_=f_: s.activation(out=kh, in_=f_, func=AF.Copy), sem="sa")
        b.wait_tok("vector", t_h)
        t_l = b.op("vector", lambda v, kl=kl, f_=f_, kh=kh: v.tensor_tensor(
            out=kl, in0=f_, in1=kh, op=OP.subtract), sem="va")
        t_s = b.op("vector", lambda v, f_=f_: v.tensor_tensor(
            out=ev[:, 0:512], in0=f_, in1=f_, op=OP.mult), sem="va")
        b.wait_tok("tensor", t_s)
        t_n = b.op("tensor", lambda t: t.matmul(psX[0:1, :], onesf[:, 0:1], ev[:, 0:512], start=True, stop=True), sem="tt")
        b.wait_tok("vector", t_n)
        b.op("vector", lambda v: v.reduce_max(sc1[0:1, 0:1], psX[0:1, :], axis=AX.X))
        if th == 0:
            b.op("vector", lambda v, j=j: v.tensor_copy(knorm2[:, j:j + 1], sc1[0:1, 0:1]), sem="va")
        else:
            b.op("vector", lambda v, j=j: v.tensor_tensor(
                out=knorm2[:, j:j + 1], in0=knorm2[:, j:j + 1], in1=sc1[0:1, 0:1], op=OP.max), sem="va")
        b.wait_tok("sync", t_l)
        w0 = th * 256
        b.dma("sync", kpay[0, j, :, w0:w0 + 256], kh.bitcast(F32), "d")
        tok = b.dma("sync", kpay[1, j, :, w0:w0 + 256], kl.bitcast(F32), "d")
        b._kv_tok = tok
        return t_r

    def evac_v(oc, th, pst, t_mm):
        j = oc
        f_ = ev[:, 0:512]
        b.wait_tok("vector", t_mm)
        t_cp = b.op("vector", lambda v, f_=f_, pst=pst: v.tensor_copy(f_, pst), sem="va")
        tok = None
        prev = [None, None]
        for sub in range(4):
            tc = th * 4 + sub
            ptr = psC[:, (sub % 2) * 512:(sub % 2) * 512 + 128]
            b.wait_tok("tensor", t_cp)
            b.wait_tok("tensor", prev[sub % 2])
            t_tr = b.op("tensor", lambda t, sub=sub, ptr=ptr, f_=f_: t.transpose(
                ptr, f_[:, sub * 128:(sub + 1) * 128], eyet[:]), sem="tt")
            vh = kvstage[:, 0:128]
            vl = kvstage[:, 128:256]
            b.wait_tok("scalar", t_tr)
            b.wait_tok("scalar", b._kv_tok)
            t_h = b.op("scalar", lambda s, ptr=ptr, vh=vh: s.activation(out=vh, in_=ptr, func=AF.Copy), sem="sa")
            b.wait_tok("vector", t_h)
            t_l = b.op("vector", lambda v, ptr=ptr, vh=vh, vl=vl: v.tensor_tensor(
                out=vl, in0=ptr, in1=vh, op=OP.subtract), sem="va")
            prev[sub % 2] = t_l
            b.wait_tok("sync", t_l)
            b.dma("sync", vpay[0, j, tc], vh.bitcast(F32), "d")
            tok = b.dma("sync", vpay[1, j, tc], vl.bitcast(F32), "d")
            b._kv_tok = tok
        return t_cp

    b.wait("tensor", "vb")
    qkv(wk, NKV, evac_k)
    qkv(wv, NKV, evac_v)

    b.wait("sync", "va")
    b.dma("sync", npay.rearrange("(a j) -> a j", a=1), knorm2[:], "d")
    b.dump("pay", pay_in)

    b.wait("gpsimd", "d")
    rec = b.sems["cc"]; rec[1] += 1
    b.on("gpsimd", lambda g, h=rec[0]: g.collective_compute(
        "AllGather", OP.bypass, replica_groups=[list(range(NCORES))],
        ins=[pay_in.opt()], outs=[pay_out.opt()]).then_inc(h, 1))

    qkv(wq, NH, evac_q, lo_eng="vector")
    b.dump("qc", QC[:])

    if stage >= 2:
        for h in range(NH):
            qsl = QC[:, h * 1024:(h + 1) * 1024]
            t_q = b.op("vector", lambda v, qsl=qsl: v.tensor_tensor(
                out=qsq[:], in0=qsl, in1=qsl, op=OP.mult), sem="va")
            b.wait_tok("tensor", t_q)
            b.op("tensor", lambda t: t.matmul(psR[0:1, :], onesf[:, 0:1], qsq[:, 0:512], start=True, stop=True))
            t_n = b.op("tensor", lambda t: t.matmul(psX[0:1, :], onesf[:, 0:1], qsq[:, 512:1024], start=True, stop=True), sem="tt")
            b.wait_tok("vector", t_n)
            b.op("vector", lambda v: v.reduce_max(sc1[0:1, 0:1], psR[0:1, :], axis=AX.X))
            b.op("vector", lambda v: v.reduce_max(sc1[0:1, 1:2], psX[0:1, :], axis=AX.X))
            b.op("vector", lambda v: v.tensor_tensor(out=sc1[0:1, 0:1], in0=sc1[0:1, 0:1], in1=sc1[0:1, 1:2], op=OP.max))
            b.op("vector", lambda v, h=h: v.tensor_copy(qmax2[:, h:h + 1], sc1[0:1, 0:1]), sem="va")

    b.wait("gpsimd", "cc")
    t_gg = b.op("gpsimd", lambda g: g.memset(sc2[0:1, 0:1], 0.0), sem="ga")
    for e_ in ("sync", "tensor", "vector", "scalar"):
        b.wait_tok(e_, t_gg)
    b.dump("gat", pay_out)

    if stage >= 2:
        t_kn = b.dma("sync", kn_all[:].rearrange("a (r j) -> a r j", r=NCORES),
                     pay_out[:, 2 * KW + 2 * VW: 2 * KW + 2 * VW + NKV], "dn")
        b.wait_tok("vector", t_kn)
        b.op("vector", lambda v: v.reduce_max(
            kmax2[:], kn_all[:].rearrange("a (r j) -> a j r", r=NCORES), axis=AX.X), sem="va")
        for h in range(NH):
            b.op("vector", lambda v, h=h: v.tensor_tensor(
                out=sc2[0:1, h:h + 1], in0=qmax2[:, h:h + 1],
                in1=kmax2[:, h // 2:h // 2 + 1], op=OP.mult),
                sem="va" if h == NH - 1 else None)
        b.wait("scalar", "va")
        t_sq2 = b.op("scalar", lambda s: s.activation(out=sc1[0:1, 0:NH], in_=sc2[0:1, 0:NH], func=AF.Sqrt), sem="sa")
        b.wait_tok("vector", t_sq2)
        t_ng = b.op("vector", lambda v: v.tensor_scalar(
            out=sc1[0:1, NH:2 * NH], in0=sc1[0:1, 0:NH], scalar1=-INV_SQRT_E,
            scalar2=None, op0=OP.mult), sem="va")
        b.wait_tok("tensor", t_ng)
        t_bc = b.op("tensor", lambda t: t.matmul(psR[:, 0:NH], onesf[0:1, :], sc1[0:1, NH:2 * NH], start=True, stop=True), sem="tt")
        b.wait_tok("vector", t_bc)
        b.op("vector", lambda v: v.tensor_copy(mh[:], psR[:, 0:NH]), sem="va")
        b.wait("scalar", "va")

        kvbuf = [kvb0, kvb1]
        flat = pay_out.rearrange("r n -> (r n)")

        def kv_dma(j, blk, tok_free):
            r = blk
            buf = kvbuf[blk % 2]
            b.wait_tok("sync", tok_free)
            base_kh = r * PAY + j * 128 * 512
            base_kl = r * PAY + KW + j * 128 * 512
            base_vh = r * PAY + 2 * KW + j * TCH * 128 * 64
            base_vl = r * PAY + 2 * KW + VW + j * TCH * 128 * 64
            ds = "dkv%d" % (blk % 2)
            b.dma("sync", buf[:, 0:1024].bitcast(F32),
                  flat[base_kh: base_kh + 128 * 512].rearrange("(p w) -> p w", p=128), ds)
            b.dma("sync", buf[:, 1024:2048].bitcast(F32),
                  flat[base_kl: base_kl + 128 * 512].rearrange("(p w) -> p w", p=128), ds)
            b.dma("sync", buf[:, 2048:3072].bitcast(F32).rearrange("p (c w) -> p c w", c=TCH),
                  flat[base_vh: base_vh + TCH * 128 * 64].rearrange("(c p w) -> p c w", c=TCH, p=128), ds)
            return b.dma("sync", buf[:, 3072:4096].bitcast(F32).rearrange("p (c w) -> p c w", c=TCH),
                         flat[base_vl: base_vl + TCH * 128 * 64].rearrange("(c p w) -> p c w", c=TCH, p=128), ds)

        NU = 2 * KCH
        psSslot = [psS0[:, 0:512], psS0[:, 512:1024], psS1[:, 0:512], psS1[:, 512:1024]]
        prev_sweep_pv = ("tq", b.sems["tq"][1])
        prev_sweep_scores = ("tq", b.sems["tq"][1])
        ctx_ch_tok = [None]
        for j in range(NKV):
            for hh in range(2):
                h = 2 * j + hh
                b.wait("tensor", "se")
                b.wait("tensor", "sh", b.sems["sh"][1])
                b.wait("scalar", "vh")
                b.wait("scalar", "vr", b.sems["vr"][1])
                b.wait("scalar", "tp", b.sems["tp"][1])
                b.wait("vector", "tp")
                b.wait("vector", "sh", b.sems["sh"][1])
                qsrc = QC[:, h * 1024:(h + 1) * 1024]
                b.wait_tok("vector", prev_sweep_scores)
                b.wait_tok("vector", ctx_ch_tok[0])
                t_z0 = b.op("vector", lambda v: v.memset(ctmp[:], 0.0), sem="vr")
                acc_tok = [t_z0, t_z0]
                t_qh = b.op("vector", lambda v, qsrc=qsrc: v.tensor_copy(qh_s, qsrc), sem="vh", nd=True)
                b.wait_tok("gpsimd", t_qh)
                t_ql = b.op("gpsimd", lambda g, qsrc=qsrc: g.tensor_tensor(
                    out=ql_s, in0=qsrc, in1=qh_s, op=OP.subtract), sem="gl", nd=True)
                b.wait_tok("tensor", t_qh)
                b.wait_tok("tensor", t_ql)
                e_tok = [None] * NU
                h_tok = [None] * NU
                l_tok = [None] * NU
                s_tok = [None] * NU
                p_tok = [None] * NU
                a_tok = [None] * NU
                d_tok = [None] * 8
                d_tok[0] = kv_dma(j, 0, prev_sweep_pv)
                d_tok[1] = kv_dma(j, 1, prev_sweep_pv)

                def kv_tiles(u):
                    c, qh2 = divmod(u, 2)
                    blk, lc = divmod(c, 8)
                    buf = kvbuf[blk % 2]
                    return (blk, qh2,
                            buf[:, 0:1024][:, lc * 128:(lc + 1) * 128],
                            buf[:, 1024:2048][:, lc * 128:(lc + 1) * 128],
                            buf[:, 2048:3072][:, lc * 128:(lc + 1) * 128],
                            buf[:, 3072:4096][:, lc * 128:(lc + 1) * 128])

                def emit_scores(u):
                    blk, qh2, Kh, Kl, Vh, Vl = kv_tiles(u)
                    b.wait_tok("tensor", d_tok[blk])
                    if u >= 4:
                        b.wait_tok("tensor", e_tok[u - 4])
                    s_tok[u] = b.mm3(psSslot[u % 4], Kh, Kl,
                                     qh_s[:, qh2 * 512:(qh2 + 1) * 512],
                                     ql_s[:, qh2 * 512:(qh2 + 1) * 512],
                                     start=True, stop=True, sem="ts")

                for u0 in range(3):
                    emit_scores(u0)
                for u in range(NU):
                    c, qh2 = divmod(u, 2)
                    blk, _, Kh, Kl, Vh, Vl = kv_tiles(u)
                    sl = u % 4
                    pb = pbuf[:, sl * 512:(sl + 1) * 512]
                    phb = phv[:, sl * 512:(sl + 1) * 512]
                    plb = pl_[:, sl * 512:(sl + 1) * 512]
                    b.wait_tok("scalar", s_tok[u])
                    if u >= 4:
                        b.wait_tok("scalar", l_tok[u - 4])
                        b.wait_tok("scalar", a_tok[u - 4])
                    e_tok[u] = b.op("scalar", lambda s, ps_=psSslot[sl], pb=pb, h=h: s.activation(
                        out=pb, in_=ps_, func=AF.Exp, bias=mh[:, h:h + 1],
                        scale=INV_SQRT_E), sem="se", nd=True)
                    b.wait_tok("scalar", e_tok[u])
                    if u >= 4:
                        b.wait_tok("scalar", p_tok[u - 4])
                        b.wait_tok("scalar", l_tok[u - 4])
                    h_tok[u] = b.op("scalar", lambda s, pb=pb, phb=phb: s.activation(
                        out=phb, in_=pb, func=AF.Copy), sem="sh", nd=True)
                    b.wait_tok("vector", h_tok[u])
                    if u >= 4:
                        b.wait_tok("vector", p_tok[u - 4])
                    l_tok[u] = b.op("vector", lambda v, pb=pb, phb=phb, plb=plb: v.tensor_tensor(
                        out=plb, in0=pb, in1=phb, op=OP.subtract), sem="vh", nd=True)
                    b.wait_tok("vector", acc_tok[qh2])
                    acc_tok[qh2] = b.op("vector", lambda v, pb=pb, qh2=qh2: v.tensor_tensor(
                        out=ctmp[:, qh2 * 512:(qh2 + 1) * 512],
                        in0=ctmp[:, qh2 * 512:(qh2 + 1) * 512], in1=pb, op=OP.add),
                        sem="vr", nd=True)
                    a_tok[u] = acc_tok[qh2]
                    b.wait_tok("tensor", h_tok[u])
                    b.wait_tok("tensor", l_tok[u])
                    first, last = (c == 0), (c == KCH - 1)
                    p_tok[u] = b.mm3(psC[:, qh2 * 512:(qh2 + 1) * 512], Vh, Vl, phb, plb,
                                     start=first, stop=last, sem="tp")
                    if qh2 == 1 and c % 8 == 7 and blk + 2 < 8:
                        d_tok[blk + 2] = kv_dma(j, blk + 2, p_tok[u])
                    if u + 3 < NU:
                        emit_scores(u + 3)
                prev_sweep_pv = p_tok[NU - 1]
                prev_sweep_scores = s_tok[NU - 1]
                b.wait_tok("tensor", acc_tok[0])
                b.wait_tok("tensor", acc_tok[1])
                b.op("tensor", lambda t: t.matmul(psR[0:1, :], onesf[:, 0:1], ctmp[:, 0:512], start=True, stop=True))
                t_rsum = b.op("tensor", lambda t: t.matmul(psX[0:1, :], onesf[:, 0:1], ctmp[:, 512:1024], start=True, stop=True), sem="tt")
                b.wait_tok("vector", t_rsum)
                b.wait_tok("vector", p_tok[NU - 1])
                b.op("vector", lambda v: v.reciprocal(ev[0:1, 0:512], psR[0:1, :]))
                t_rc = b.op("vector", lambda v: v.reciprocal(ev[0:1, 512:1024], psX[0:1, :]), sem="va")
                b.wait_tok("tensor", t_rc)
                t_b0 = b.op("tensor", lambda t: t.matmul(psX[:, :], onesf[0:1, :], ev[0:1, 0:512], start=True, stop=True), sem="tt")
                b.wait_tok("vector", t_b0)
                t_c0 = b.op("vector", lambda v: v.tensor_copy(inv_rs[:, 0:512], psX[:]), sem="va")
                b.wait_tok("tensor", t_c0)
                t_b1 = b.op("tensor", lambda t: t.matmul(psX[:, :], onesf[0:1, :], ev[0:1, 512:1024], start=True, stop=True), sem="tt")
                b.wait_tok("vector", t_b1)
                b.op("vector", lambda v: v.tensor_copy(inv_rs[:, 512:1024], psX[:]), sem="va")
                t_nm = b.op("vector", lambda v: v.tensor_tensor(out=ctmp[:], in0=psC[:], in1=inv_rs[:], op=OP.mult), sem="va")
                ch = Abf[:, h * 2048: h * 2048 + 1024]
                cl = Abf[:, h * 2048 + 1024: h * 2048 + 2048]
                b.wait_tok("scalar", t_nm)
                t_ch = b.op("scalar", lambda s, ch=ch: s.activation(out=ch, in_=ctmp[:], func=AF.Copy), sem="sa", nd=True)
                ctx_ch_tok[0] = t_ch
                b.wait_tok("vector", t_ch)
                b.op("vector", lambda v, ch=ch, cl=cl: v.tensor_tensor(
                    out=cl, in0=ctmp[:], in1=ch, op=OP.subtract), sem="vb", nd=True)

        b.dump("ctx", A[:])

    if stage >= 3:
        prev_ev_tok = [None] * TCH
        prev_x_tok = None
        wo_f = {}

        def wo_fetch(t):
            if t >= 64 or t in wo_f:
                return
            wo_f[t] = (weight_half(wo[t // FCH, t % FCH], 512, t % 3), t % 3)

        wo_fetch(0); wo_fetch(1); wo_fetch(2)
        for oh in range(4):
            for fc in range(FCH):
                tno = oh * FCH + fc
                (hs, ls, t_v, t_g), slot = wo_f[tno]
                b.wait_tok("tensor", t_v)
                b.wait_tok("tensor", t_g)
                t_mm = None
                for qc in range(TCH):
                    ch = Abf[:, fc * 2048 + qc * 128: fc * 2048 + qc * 128 + 128]
                    cl = Abf[:, fc * 2048 + 1024 + qc * 128: fc * 2048 + 1024 + qc * 128 + 128]
                    if fc == 0:
                        b.wait_tok("tensor", prev_ev_tok[qc])
                    t_mm = b.mm3(bank[qc], ch, cl, hs[:, 0:512], ls[:, 0:512],
                                 start=(fc == 0), stop=(fc == FCH - 1),
                                 sem="tq" if qc == TCH - 1 else None)
                wslot_consumed(slot, t_mm)
                wo_fetch(tno + 3)
            for qc in range(TCH):
                b.wait_tok("sync", prev_x_tok)
                t_x = b.dma("sync", ev[:, 0:512],
                            x_in[qc * 128:(qc + 1) * 128, oh * 512:(oh + 1) * 512], "d")
                b.wait_tok("vector", t_x)
                b.wait("vector", "tq")
                prev_ev_tok[qc] = b.op("vector", lambda v, qc=qc, oh=oh: v.tensor_tensor(
                    out=x2[:, qc * 2048 + oh * 512: qc * 2048 + (oh + 1) * 512],
                    in0=bank[qc], in1=ev[:, 0:512], op=OP.add), sem="vb", nd=True)
                prev_x_tok = prev_ev_tok[qc]
        b.dump("x2", B[:])

    if stage >= 4:
        for tc in range(TCH):
            src = x2[:, tc * 2048:(tc + 1) * 2048]
            b.wait("vector", "vb")
            b.wait("scalar", "vb", b.sems["vb"][1])
            t_ap = emit_ln(src)
            emit_transpose_splits(tc, Abf, t_ap)

        hs1, ls1, t_v, t_g = weight_half(
            rw1.rearrange("f p r -> p f r"), FCH * RHID, 0)
        b.wait("tensor", "vb")
        b.wait_tok("tensor", t_v)
        b.wait_tok("tensor", t_g)
        for th in range(2):
            pst = psR if th == 0 else psX
            t_mm = None
            for fc in range(FCH):
                wh_ = hs1[:, fc * RHID:(fc + 1) * RHID]
                wl_ = ls1[:, fc * RHID:(fc + 1) * RHID]
                rh_ = lnh_of(Abf, fc, th * 512, th * 512 + 512)
                rl_ = lnl_of(Abf, fc, th * 512, th * 512 + 512)
                t_mm = b.mm3(pst[0:RHID, :], wh_, wl_, rh_, rl_, start=(fc == 0),
                             stop=(fc == FCH - 1), sem="tq" if fc == FCH - 1 else None)
            wslot_consumed(0, t_mm)
            b.wait_tok("scalar", t_mm)
            b.op("scalar", lambda s, pst=pst, th=th: s.activation(
                out=h_fm[:, th * 512:(th + 1) * 512], in_=pst[0:RHID, :],
                func=AF.Silu, bias=rb1t[:], scale=1.0), sem="sa")
        b.wait("vector", "sa")
        t_hh = b.op("vector", lambda v: v.tensor_copy(h_h[:], h_fm[:]), sem="va")
        b.wait_tok("gpsimd", t_hh)
        t_hl = b.op("gpsimd", lambda g: g.tensor_tensor(out=h_l[:], in0=h_fm[:], in1=h_h[:], op=OP.subtract), sem="ga")
        b.wait_tok("sync", wslot["tok"][1])
        if wslot["split"][1] is not None:
            b.wait_tok("sync", wslot["split"][1][0])
            b.wait_tok("sync", wslot["split"][1][1])
        t_w2 = b.dma("sync", wstage[0:RHID, 1024:1056], rw2, "dw1")
        b.wait_tok("vector", t_w2)
        t_r2h = b.op("vector", lambda v: v.tensor_copy(r2h[:], wstage[0:RHID, 1024:1056]), sem="va")
        b.wait_tok("gpsimd", t_r2h)
        t_r2l = b.op("gpsimd", lambda g: g.tensor_tensor(out=r2l[:], in0=wstage[0:RHID, 1024:1056], in1=r2h[:], op=OP.subtract), sem="ga")
        wslot["split"][1] = (t_r2h, t_r2l)
        for tk in (t_hh, t_r2h):
            b.wait_tok("tensor", tk)
        for tk in (t_hl, t_r2l):
            b.wait_tok("tensor", tk)
        for th in range(2):
            pst = psR if th == 0 else psX
            t_mm = b.mm3(pst[0:32, :], r2h[:], r2l[:],
                         h_h[:, th * 512:(th + 1) * 512],
                         h_l[:, th * 512:(th + 1) * 512],
                         start=True, stop=True, sem="tq")
            b.wait_tok("vector", t_mm)
            b.op("vector", lambda v, pst=pst, th=th: v.tensor_scalar(
                out=lg_fm[:, th * 512:(th + 1) * 512], in0=pst[0:16, :],
                scalar1=rbgt[:], scalar2=None, op0=OP.add), sem="va")
        b.wait("tensor", "va")
        prev_cf = None
        for tc in range(TCH):
            t_tr = b.op("tensor", lambda t, tc=tc: t.transpose(
                psR[:, 0:16], lg_fm[:, tc * 128:(tc + 1) * 128], eyet[0:16, 0:16]), sem="tt")
            b.wait_tok("vector", t_tr)
            b.wait_tok("vector", prev_cf)
            b.op("vector", lambda v: v.tensor_copy(lg_tm[:], psR[:, 0:16]))
            b.op("vector", lambda v: v.reduce_max(sc1[:, 0:1], lg_tm[:], axis=AX.X))
            b.op("vector", lambda v: v.tensor_scalar(out=ind1[:], in0=lg_tm[:],
                 scalar1=sc1[:, 0:1], scalar2=None, op0=OP.is_equal))
            b.op("vector", lambda v: v.scalar_tensor_tensor(out=msk1[:], in0=ind1[:],
                 scalar=-1e30, in1=lg_tm[:], op0=OP.mult, op1=OP.add))
            b.op("vector", lambda v: v.reduce_max(sc1[:, 1:2], msk1[:], axis=AX.X))
            b.op("vector", lambda v: v.tensor_scalar(out=msk2[:], in0=msk1[:],
                 scalar1=sc1[:, 1:2], scalar2=None, op0=OP.is_equal))
            t_d1 = b.op("vector", lambda v: v.tensor_tensor(out=sc1[:, 2:3], in0=sc1[:, 1:2], in1=sc1[:, 0:1], op=OP.subtract), sem="va")
            b.wait_tok("scalar", t_d1)
            t_z = b.op("scalar", lambda s: s.activation(out=sc1[:, 3:4], in_=sc1[:, 2:3], func=AF.Exp), sem="sa")
            b.wait_tok("vector", t_z)
            b.op("vector", lambda v: v.tensor_scalar(out=sc1[:, 4:5], in0=sc1[:, 3:4], scalar1=1.0, scalar2=None, op0=OP.add))
            b.op("vector", lambda v: v.reciprocal(sc1[:, 5:6], sc1[:, 4:5]))
            b.op("vector", lambda v: v.tensor_tensor(out=sc1[:, 6:7], in0=sc1[:, 3:4], in1=sc1[:, 5:6], op=OP.mult))
            b.op("vector", lambda v: v.tensor_scalar(out=cf_tm[:], in0=ind1[:], scalar1=sc1[:, 5:6], scalar2=None, op0=OP.mult))
            t_cf = b.op("vector", lambda v: v.scalar_tensor_tensor(
                out=cf_tm[:], in0=msk2[:], scalar=sc1[:, 6:7], in1=cf_tm[:],
                op0=OP.mult, op1=OP.add), sem="va")
            b.wait_tok("tensor", t_cf)
            t_tr2 = b.op("tensor", lambda t: t.transpose(psX[0:16, 0:128], cf_tm[:], eyet[:]), sem="tt")
            b.wait_tok("vector", t_tr2)
            prev_cf = b.op("vector", lambda v, tc=tc: v.tensor_copy(
                cf_fm[:, tc * 128:(tc + 1) * 128], psX[0:16, 0:128]), sem="va")
        b.dump("cf", cf_fm[:])

        b.wait_tok("tensor", prev_cf)
        for ec in range(2):
            for th in range(2):
                pst = psR if th == 0 else psX
                t_mm = b.op("tensor", lambda t, ec=ec, th=th, pst=pst: t.matmul(
                    pst[:, :], ematt[:, ec * 128:(ec + 1) * 128],
                    cf_fm[:, th * 512:(th + 1) * 512], start=True, stop=True), sem="tt")
                b.wait_tok("vector", t_mm)
                b.op("vector", lambda v, th=th, pst=pst: v.tensor_copy(
                    cfe[:, th * 512:(th + 1) * 512], pst[:, :]), sem="va")
            halves = []
            for hf in range(2):
                slot = hf
                hs, ls, t_v, t_g = weight_half(
                    wa[ec, hf].rearrange("f p o -> p f o"), 1024, slot)
                halves.append((hs, ls, t_v, t_g))
            for th in range(2):
                pst = psS0[:, th * 512:(th + 1) * 512] if ec == 0 else psS1[:, th * 512:(th + 1) * 512]
                t_mm = None
                for fc in range(FCH):
                    hf, fcl = divmod(fc, 8)
                    hs, ls, t_v, t_g = halves[hf]
                    if fcl == 0 and th == 0:
                        b.wait_tok("tensor", t_v)
                        b.wait_tok("tensor", t_g)
                    wh_ = hs[:, fcl * 128:(fcl + 1) * 128]
                    wl_ = ls[:, fcl * 128:(fcl + 1) * 128]
                    rh_ = lnh_of(Abf, fc, th * 512, th * 512 + 512)
                    rl_ = lnl_of(Abf, fc, th * 512, th * 512 + 512)
                    t_mm = b.mm3(pst, wh_, wl_, rh_, rl_, start=(fc == 0),
                                 stop=(fc == FCH - 1), sem="tq" if fc == FCH - 1 else None)
                wslot_consumed(0, t_mm)
                wslot_consumed(1, t_mm)
                b.wait_tok("vector", t_mm)
                b.wait("vector", "va")
                t_xa = b.op("vector", lambda v, pst=pst, th=th: v.tensor_tensor(
                    out=xa_t[:, th * 512:(th + 1) * 512], in0=pst,
                    in1=cfe[:, th * 512:(th + 1) * 512], op=OP.mult), sem="va")
                b.wait_tok("scalar", t_xa)
                t_sh = b.op("scalar", lambda s, ec=ec, th=th: s.activation(
                    out=wxh[:, ec * 1024 + th * 512: ec * 1024 + (th + 1) * 512],
                    in_=xa_t[:, th * 512:(th + 1) * 512], func=AF.Copy), sem="sa")
                b.wait_tok("vector", t_sh)
                b.op("vector", lambda v, ec=ec, th=th: v.tensor_tensor(
                    out=wxl[:, ec * 1024 + th * 512: ec * 1024 + (th + 1) * 512],
                    in0=xa_t[:, th * 512:(th + 1) * 512],
                    in1=wxh[:, ec * 1024 + th * 512: ec * 1024 + (th + 1) * 512],
                    op=OP.subtract), sem="vb")

        b.wait("tensor", "vb")
        prev_ev_tok = [None] * TCH
        prev_out_tok = [None, None]
        ff_f = {}

        def ff_fetch(t):
            if t >= 72 or t in ff_f:
                return
            ohx, u = divmod(t, 18)
            ap = wff[ohx, u] if u < FCH else wb[ohx, u - FCH]
            ff_f[t] = (weight_half(ap, 512, t % 3), t % 3)

        ff_fetch(0); ff_fetch(1); ff_fetch(2)
        for oh in range(4):
            t_fb = b.dma("sync", fbias[:], ffb[oh:oh + 1, :], "dfb")
            for fc in range(FCH):
                tno = oh * 18 + fc
                (hs, ls, t_v, t_g), slot = ff_f[tno]
                b.wait_tok("tensor", t_v)
                b.wait_tok("tensor", t_g)
                t_mm = None
                for qc in range(TCH):
                    lh = lnh_of(Abf, fc, qc * 128, qc * 128 + 128)
                    ll = lnl_of(Abf, fc, qc * 128, qc * 128 + 128)
                    if fc == 0:
                        b.wait_tok("tensor", prev_ev_tok[qc])
                    t_mm = b.mm3(bank[qc], lh, ll, hs[:, 0:512], ls[:, 0:512],
                                 start=(fc == 0), stop=False,
                                 sem="tq" if qc == TCH - 1 else None)
                wslot_consumed(slot, t_mm)
                ff_fetch(tno + 3)
            for ec in range(2):
                tno = oh * 18 + FCH + ec
                (hs, ls, t_v, t_g), slot = ff_f[tno]
                b.wait_tok("tensor", t_v)
                b.wait_tok("tensor", t_g)
                t_mm = None
                for qc in range(TCH):
                    xh_ = wxh[:, ec * 1024 + qc * 128: ec * 1024 + qc * 128 + 128]
                    xl_ = wxl[:, ec * 1024 + qc * 128: ec * 1024 + qc * 128 + 128]
                    t_mm = b.mm3(bank[qc], xh_, xl_, hs[:, 0:512], ls[:, 0:512],
                                 start=False, stop=False,
                                 sem="tq" if qc == TCH - 1 else None)
                wslot_consumed(slot, t_mm)
                ff_fetch(tno + 3)
            b.wait_tok("tensor", t_fb)
            t_last = [None] * TCH
            for qc in range(TCH):
                t_last[qc] = b.op("tensor", lambda t, qc=qc: t.matmul(
                    bank[qc], onesf[0:1, :], fbias[:], start=False, stop=True), sem="tq")
            for qc in range(TCH):
                oslot = qc % 2
                b.wait_tok("vector", t_last[qc])
                b.wait_tok("vector", prev_out_tok[oslot])
                t_ev = b.op("vector", lambda v, qc=qc, oh=oh, oslot=oslot: v.tensor_tensor(
                    out=outt[:, oslot * 512:(oslot + 1) * 512],
                    in0=bank[qc],
                    in1=x2[:, qc * 2048 + oh * 512: qc * 2048 + (oh + 1) * 512],
                    op=OP.add), sem="vb", nd=True)
                prev_ev_tok[qc] = t_ev
                b.wait_tok("sync", t_ev)
                prev_out_tok[oslot] = b.dma(
                    "sync", y_out[qc * 128:(qc + 1) * 128, oh * 512:(oh + 1) * 512],
                    outt[:, oslot * 512:(oslot + 1) * 512], "dout%d" % oslot)

    if stage < 4:
        t_z = b.op("vector", lambda v: v.memset(ev[:, 0:512], 0.0), sem="va")
        b.wait_tok("sync", t_z)
        for tc in range(TCH):
            for oh in range(4):
                b.dma("sync", y_out[tc * 128:(tc + 1) * 128, oh * 512:(oh + 1) * 512],
                      ev[:, 0:512], "dout")

    b.wait("sync", "dout")
    b.wait("sync", "dout0")
    b.wait("sync", "dout1")
    _replay(b)
    return b


def _replay(b):
    blk = b.stack.enter_context(b.nc.Block())
    by_eng = {"sync": [], "tensor": [], "vector": [], "scalar": [], "gpsimd": []}
    for eng, fn in b.ops:
        by_eng[eng].append(fn)

    @blk.sync
    def _(e):
        for fn in by_eng["sync"]:
            fn(e)

    @blk.tensor
    def _(e):
        for fn in by_eng["tensor"]:
            fn(e)

    @blk.vector
    def _(e):
        for fn in by_eng["vector"]:
            fn(e)

    @blk.scalar
    def _(e):
        for fn in by_eng["scalar"]:
            fn(e)

    @blk.gpsimd
    def _(e):
        for fn in by_eng["gpsimd"]:
            fn(e)

    b.stack.close()



def _prep_weights(inp):
    f32 = np.float32
    WQ = np.asarray(inp["L0_attn_W_Q"], f32)
    WK = np.asarray(inp["L0_attn_W_K"], f32)
    WV = np.asarray(inp["L0_attn_W_V"], f32)
    WO = np.asarray(inp["L0_attn_W_O"], f32)
    permq = (np.arange(HD)[None, :] * NH + np.arange(NH)[:, None]).reshape(-1)
    permk = (np.arange(HD)[None, :] * NKV + np.arange(NKV)[:, None]).reshape(-1)

    def lhsT_tiles(Wp, noc):
        Wt = Wp.T
        t = Wt.reshape(2, 8, 128, noc, 128).transpose(3, 0, 1, 2, 4)
        return np.ascontiguousarray(t)

    wq_t = lhsT_tiles(WQ[permq], NH)
    wk_t = lhsT_tiles(WK[permk], NKV)
    wv_t = lhsT_tiles(WV[permk], NKV)
    wo_t = np.ascontiguousarray(WO.T.reshape(FCH, 128, 4, 512).transpose(2, 0, 1, 3))
    wff_t = np.ascontiguousarray(np.asarray(inp["L0_ffn_W"], f32).T
                                 .reshape(FCH, 128, 4, 512).transpose(2, 0, 1, 3))
    Amat = np.asarray(inp["L0_ffn_A"], f32).reshape(ER, E)
    wa_t = np.ascontiguousarray(Amat.T.reshape(2, 8, 128, 2, 128).transpose(3, 0, 1, 2, 4))
    Bmat = np.asarray(inp["L0_ffn_B"], f32).transpose(0, 2, 1).reshape(ER, E)
    wb_t = np.ascontiguousarray(Bmat.reshape(2, 128, 4, 512).transpose(2, 0, 1, 3))
    rw1_t = np.ascontiguousarray(np.asarray(inp["L0_ffn_rw1"], f32).T.reshape(FCH, 128, RHID))
    rw2_t = np.ascontiguousarray(np.asarray(inp["L0_ffn_rw2"], f32).T)
    ffb = np.ascontiguousarray(np.asarray(inp["L0_ffn_bias"], f32).reshape(4, 512))
    rb1 = np.ascontiguousarray(np.asarray(inp["L0_ffn_rb1"], f32).reshape(RHID, 1))
    rbg = np.ascontiguousarray((np.asarray(inp["L0_ffn_rb2"], f32)[:NEXP]
                                + np.asarray(inp["L0_ffn_gates"], f32)).reshape(NEXP, 1))
    emat = np.zeros((NEXP, ER), f32)
    for e_ in range(NEXP):
        emat[e_, e_ * RANK:(e_ + 1) * RANK] = 1.0
    return dict(wq=wq_t, wk=wk_t, wv=wv_t, wo=wo_t, wff=wff_t, wa=wa_t, wb=wb_t,
                rw1=rw1_t, rw2=rw2_t, ffb=ffb, rb1=rb1, rbg=rbg,
                eye=np.eye(128, dtype=f32), emat=emat)


def _rope_tables(core):
    d = np.arange(HD)
    invfreq = 10000.0 ** (-(d % 64).astype(np.float64) / 64.0)
    pos = np.arange(core * T, (core + 1) * T, dtype=np.float64)
    ang = invfreq[:, None] * pos[None, :]
    cos = np.cos(ang).astype(np.float32)
    sin = np.sin(ang).astype(np.float32)
    sinm = sin.copy()
    sinm[:64] = -sin[:64]
    return np.ascontiguousarray(cos), np.ascontiguousarray(sinm)


def make_in_maps(inputs):
    x = np.asarray(inputs["L0_attn_x"], np.float32).reshape(S, E)
    w = _prep_weights(inputs)
    in_maps = []
    for c in range(NCORES):
        cos, sinm = _rope_tables(c)
        m = dict(w)
        m["x"] = np.ascontiguousarray(x[c * T:(c + 1) * T])
        m["cosq"] = cos
        m["sinm"] = sinm
        in_maps.append(m)
    return in_maps


def kernel(**inputs):
    b = build()
    in_maps = make_in_maps(inputs)
    res = run_bass_kernel_spmd(b.nc, in_maps, core_ids=list(range(NCORES)))
    out = np.concatenate([r["y"] for r in res.results], axis=0)
    return out.reshape(1, S, E)


# version 07
# speedup vs baseline: 1.3906x; 1.3906x over previous
from contextlib import ExitStack

import numpy as np

import concourse.bass as bass
import concourse.mybir as mybir
from concourse.bass_utils import run_bass_kernel_spmd

F32 = mybir.dt.float32
BF16 = mybir.dt.bfloat16
AF = mybir.ActivationFunctionType
OP = mybir.AluOpType
AX = mybir.AxisListType

NCORES = 8
S = 8192
T = S // NCORES
E = 2048
HD = 128
NH = 16
NKV = 8
FCH = E // 128
TCH = T // 128
KCH = S // 128
RANK = 16
NEXP = 16
ER = RANK * NEXP
RHID = 64
INV_SQRT_E = float(np.float32(1.0 / np.sqrt(np.float64(E))))
LN_EPS = 1e-5

KW = NKV * 128 * (T // 2)
VW = NKV * TCH * 128 * (HD // 2)
PAY = 2 * KW + 2 * VW + NKV

STAGE = 4


class Builder:
    def __init__(self, stage=STAGE, debug=()):
        self.stage = stage
        self.debug = list(debug)
        self.nc = bass.Bass("TRN2", target_bir_lowering=False, debug=False)
        self.stack = ExitStack()
        self.ops = []
        self.sems = {}
        self.dbg = {}

    def sem(self, name):
        h = self.stack.enter_context(self.nc.semaphore(name))
        self.sems[name] = [h, 0]

    def sb(self, name, shape, dt=F32):
        return self.stack.enter_context(self.nc.sbuf_tensor(name, shape, dt))

    def ps(self, name, shape):
        return self.stack.enter_context(self.nc.psum_tensor(name, shape, F32))

    def on(self, eng, fn):
        self.ops.append((eng, fn))

    def wait(self, eng, name, value=None):
        h = self.sems[name][0]
        v = self.sems[name][1] if value is None else value
        if v > 0:
            self.on(eng, lambda e, h=h, v=v: e.wait_ge(h, v))

    def wait_tok(self, eng, tok):
        if tok is not None:
            self.wait(eng, tok[0], tok[1])

    DRAIN_ENGS = ("vector", "scalar", "gpsimd")

    def op(self, eng, fn, sem=None, n=1, nd=False):
        if eng in self.DRAIN_ENGS and not nd:
            self.on(eng, lambda e: e.drain())
        if sem is None:
            self.on(eng, fn)
            return None
        rec = self.sems[sem]
        rec[1] += n
        inc = (rec[0], n)
        self.on(eng, lambda e, fn=fn, inc=inc: fn(e).then_inc(*inc))
        return (sem, rec[1])

    def dma(self, eng, out, in_, sem):
        return self.op(eng, lambda e, out=out, in_=in_: e.dma_start(out=out, in_=in_),
                       sem=sem, n=16)

    def mm3(self, out, ah, al, bh, bl, start, stop, sem=None):
        self.on("tensor", lambda t, o=out, a=ah, b_=bh, s=start: t.matmul(o, a, b_, start=s, stop=False))
        self.on("tensor", lambda t, o=out, a=ah, b_=bl: t.matmul(o, a, b_, start=False, stop=False))
        return self.op("tensor", lambda t, o=out, a=al, b_=bh, p=stop: t.matmul(o, a, b_, start=False, stop=p), sem=sem)

    def dump(self, name, ap):
        if name not in self.debug:
            return
        d = self.nc.dram_tensor("dbg_" + name, list(ap.shape), ap.dtype,
                                kind="ExternalOutput").ap()
        self.dbg[name] = d
        for e_ in self.sems:
            if e_ not in ("dout",):
                self.wait("sync", e_)
        self.dma("sync", d, ap, "dout")


def build(stage=STAGE, debug=()):
    b = Builder(stage, debug)
    nc = b.nc

    din = lambda n, s: nc.dram_tensor(n, s, F32, kind="ExternalInput").ap()
    x_in = din("x", [T, E])
    wq = din("wq", [NH, 2, 8, 128, 128])
    wk = din("wk", [NKV, 2, 8, 128, 128])
    wv = din("wv", [NKV, 2, 8, 128, 128])
    wo = din("wo", [4, FCH, 128, 512])
    wff = din("wff", [4, FCH, 128, 512])
    wa = din("wa", [2, 2, 8, 128, 128])
    wb = din("wb", [4, 2, 128, 512])
    rw1 = din("rw1", [FCH, 128, RHID])
    rw2 = din("rw2", [RHID, 32])
    ffb = din("ffb", [4, 512])
    rb1 = din("rb1", [RHID, 1])
    rbg = din("rbg", [NEXP, 1])
    eye = din("eye", [128, 128])
    cosq = din("cosq", [128, T])
    sinm = din("sinm", [128, T])
    emat = din("emat", [16, ER])
    y_out = nc.dram_tensor("y", [T, E], F32, kind="ExternalOutput").ap()
    pay_in = nc.dram_tensor("pay_in", [PAY], F32).ap()
    pay_out = nc.dram_tensor("pay_out", [NCORES, PAY], F32, addr_space="Shared").ap()

    kpay = pay_in[0:2 * KW].rearrange("(s j p w) -> s j p w", s=2, j=NKV, p=128)
    vpay = pay_in[2 * KW:2 * KW + 2 * VW].rearrange("(s j c p w) -> s j c p w",
                                                    s=2, j=NKV, c=TCH, p=128)
    npay = pay_in[2 * KW + 2 * VW: 2 * KW + 2 * VW + NKV]

    A = b.sb("arenaA", [128, 16384])
    B = b.sb("arenaB", [128, 16384])
    X = b.sb("arenaX", [128, 2048])
    Abf = A[:].bitcast(BF16)
    Bbf = B[:].bitcast(BF16)

    xt = A[:, 0:2048]
    lnt = X[:, 0:2048]
    QC = A
    def lnh_of(abf, fc, t0, t1):
        return abf[:, fc * 2048 + t0: fc * 2048 + t1]
    def lnl_of(abf, fc, t0, t1):
        return abf[:, fc * 2048 + 1024 + t0: fc * 2048 + 1024 + t1]

    pbuf = B[:, 0:2048]
    ph_ = Bbf[:, 4096:8192]
    pl_ = Bbf[:, 6144:8192]
    phv = Bbf[:, 4096:6144]
    kvb0 = Bbf[:, 8192:12288]
    kvb1 = Bbf[:, 12288:16384]
    qh_s = Bbf[:, 16384:17408]
    ql_s = Bbf[:, 17408:18432]
    inv_rs = B[:, 9216:10240]
    ctmp = B[:, 10240:11264]
    qsq = B[:, 11264:12288]
    x2 = B

    eyet = b.sb("eyet", [128, 128])
    cosqt = b.sb("cosqt", [128, T])
    sinmt = b.sb("sinmt", [128, T])
    onesf = b.sb("onesf", [128, 128])
    onesb = b.sb("onesb", [128, 64], BF16)
    ematt = b.sb("ematt", [16, ER])
    rb1t = b.sb("rb1t", [RHID, 1])
    rbgt = b.sb("rbgt", [NEXP, 1])
    fbias = b.sb("fbias", [1, 512])
    st1 = b.sb("st1", [128, 8])
    sc1 = b.sb("sc1", [128, 32])
    sc2 = b.sb("sc2", [128, 32])
    knorm2 = b.sb("knorm2", [1, NKV])
    kn_all = b.sb("kn_all", [1, NKV * NCORES])
    kmax2 = b.sb("kmax2", [1, NKV])
    qmax2 = b.sb("qmax2", [1, NH])
    mh = b.sb("mh", [128, NH])
    ev = b.sb("ev", [128, 1024])
    kvstage = b.sb("kvstage", [128, 1024], BF16)
    wstage = b.sb("wstage", [128, 3072])
    wbh = b.sb("wbh", [128, 3072], BF16)
    wbl = b.sb("wbl", [128, 3072], BF16)
    h_fm = b.sb("h_fm", [RHID, T])
    h_h = b.sb("h_h", [RHID, T], BF16)
    h_l = b.sb("h_l", [RHID, T], BF16)
    r2h = b.sb("r2h", [RHID, 32], BF16)
    r2l = b.sb("r2l", [RHID, 32], BF16)
    lg_fm = b.sb("lg_fm", [16, T])
    lg_tm = b.sb("lg_tm", [128, 16])
    msk1 = b.sb("msk1", [128, 16])
    msk2 = b.sb("msk2", [128, 16])
    ind1 = b.sb("ind1", [128, 16])
    cf_tm = b.sb("cf_tm", [128, 16])
    cf_fm = b.sb("cf_fm", [16, T])
    wxh = b.sb("wxh", [128, 2 * T], BF16)
    wxl = b.sb("wxl", [128, 2 * T], BF16)
    outt = b.sb("outt", [128, 1024])
    cfe = X[:, 0:1024]
    xa_t = X[:, 1024:2048]

    psS0 = b.ps("psS0", [128, 1024])
    psS1 = b.ps("psS1", [128, 1024])
    psC = b.ps("psC", [128, 1024])
    psR = b.ps("psR", [128, 512])
    psX = b.ps("psX", [128, 512])
    bank = [psS0[:, 0:512], psS0[:, 512:1024], psS1[:, 0:512], psS1[:, 512:1024],
            psC[:, 0:512], psC[:, 512:1024], psR[:, :], psX[:, :]]

    for s in ["d", "dw0", "dw1", "dw2", "dkv0", "dkv1", "dout", "dout0", "dout1", "vr",
              "dn", "dfb", "dx0", "dx1", "tt", "tq", "ts", "tp", "va", "vh", "vb", "sa", "se", "sh", "ga", "gl", "cc"]:
        b.sem(s)

    for dst, src in [(eyet[:], eye), (cosqt[:], cosq), (sinmt[:], sinm),
                     (ematt[:], emat), (rb1t[:], rb1), (rbgt[:], rbg)]:
        b.dma("sync", dst, src, "d")
    b.wait("vector", "d")
    b.op("vector", lambda v: v.memset(onesf[:], 1.0))
    b.op("vector", lambda v: v.tensor_copy(onesb[:], onesf[:, 0:64]), sem="va")

    def emit_ln(src_tile, lnt=None, sq_guard=-1):
        if lnt is None:
            lnt = X[:, 0:2048]
        if sq_guard == -1:
            b.wait("scalar", "tt")
        else:
            b.wait_tok("scalar", sq_guard)
        b.op("vector", lambda v, s_=src_tile: v.reduce_sum(st1[:, 0:1], s_, axis=AX.X))
        t_nm = b.op("vector", lambda v: v.tensor_scalar(
            out=st1[:, 1:2], in0=st1[:, 0:1], scalar1=-1.0 / E, scalar2=None,
            op0=OP.mult), sem="va")
        b.wait_tok("scalar", t_nm)
        t_sq = b.op("scalar", lambda s, s_=src_tile, lnt=lnt: s.activation(
            out=lnt, in_=s_, func=AF.Square, bias=st1[:, 1:2], scale=1.0,
            accum_out=st1[:, 2:3]), sem="sa")
        b.wait_tok("vector", t_sq)
        b.op("vector", lambda v: v.tensor_scalar(
            out=st1[:, 3:4], in0=st1[:, 2:3], scalar1=1.0 / E, scalar2=LN_EPS,
            op0=OP.mult, op1=OP.add), sem="va")
        b.wait("scalar", "va")
        t_rt = b.op("scalar", lambda s: s.activation(
            out=st1[:, 4:5], in_=st1[:, 3:4], func=AF.Sqrt), sem="sa")
        b.wait_tok("vector", t_rt)
        b.op("vector", lambda v: v.reciprocal(st1[:, 5:6], st1[:, 4:5]))
        b.op("vector", lambda v: v.tensor_tensor(out=st1[:, 6:7], in0=st1[:, 5:6], in1=st1[:, 5:6], op=OP.mult))
        b.op("vector", lambda v: v.tensor_tensor(out=st1[:, 6:7], in0=st1[:, 6:7], in1=st1[:, 3:4], op=OP.mult))
        b.op("vector", lambda v: v.tensor_scalar(out=st1[:, 6:7], in0=st1[:, 6:7], scalar1=-0.5, scalar2=1.5, op0=OP.mult, op1=OP.add))
        b.op("vector", lambda v: v.tensor_tensor(out=st1[:, 7:8], in0=st1[:, 5:6], in1=st1[:, 6:7], op=OP.mult))
        return b.op("vector", lambda v, s_=src_tile, lnt=lnt: v.tensor_scalar(
            out=lnt, in0=s_, scalar1=st1[:, 1:2], scalar2=st1[:, 7:8],
            op0=OP.add, op1=OP.mult), sem="va")

    def emit_transpose_splits(tc, dst_bf, apply_tok, lnt=None):
        if lnt is None:
            lnt = X[:, 0:2048]
        toks = [None] * FCH
        t_tr = None
        for fc in range(FCH):
            slot = fc % 2
            pst = psS0[:, slot * 512: slot * 512 + 128]
            b.wait_tok("tensor", apply_tok)
            if fc >= 2:
                b.wait_tok("tensor", toks[fc - 2])
            t_tr = b.op("tensor", lambda t, fc=fc, pst=pst, lnt=lnt: t.transpose(
                pst, lnt[:, fc * 128:(fc + 1) * 128], eyet[:]), sem="tt")
            hs = lnh_of(dst_bf, fc, tc * 128, tc * 128 + 128)
            ls = lnl_of(dst_bf, fc, tc * 128, tc * 128 + 128)
            b.wait_tok("scalar", t_tr)
            t_hi = b.op("scalar", lambda s, pst=pst, hs=hs: s.activation(
                out=hs, in_=pst, func=AF.Copy), sem="sa", nd=True)
            b.wait_tok("vector", t_hi)
            toks[fc] = b.op("vector", lambda v, pst=pst, hs=hs, ls=ls: v.tensor_tensor(
                out=ls, in0=pst, in1=hs, op=OP.subtract), sem="vb", nd=True)
        return toks[-1], t_tr

    ap_hist = [None, None]
    tt_hist = [None, None]
    for tc in range(TCH):
        sl_ = tc % 2
        xt_t = A[:, sl_ * 2048:(sl_ + 1) * 2048]
        lnt_t = A[:, 4096 + sl_ * 2048: 4096 + (sl_ + 1) * 2048]
        b.wait_tok("sync", ap_hist[sl_])
        t_x = b.dma("sync", xt_t, x_in.rearrange("(c p) e -> c p e", p=128)[tc], "dx%d" % sl_)
        b.wait_tok("vector", t_x)
        b.wait_tok("scalar", t_x)
        t_ap = emit_ln(xt_t, lnt=lnt_t, sq_guard=tt_hist[sl_])
        ap_hist[sl_] = t_ap
        _, tt_hist[sl_] = emit_transpose_splits(tc, Bbf, t_ap, lnt=lnt_t)

    b.dump("ln1b", B[:])

    if stage < 1:
        t_z = b.op("vector", lambda v: v.memset(ev[:, 0:512], 0.0), sem="va")
        b.wait_tok("sync", t_z)
        for tc in range(TCH):
            for oh in range(4):
                b.dma("sync", y_out[tc * 128:(tc + 1) * 128, oh * 512:(oh + 1) * 512],
                      ev[:, 0:512], "dout")
        b.wait("sync", "dout")
        _replay(b)
        return b

    b._kv_tok = None

    wslot = {"tok": [None, None, None], "split": [None, None, None]}

    def weight_half(dram_ap, ncols, slot, free_tok=-1, split_toks=-1, lo_eng="gpsimd"):
        sl = slice(slot * 1024, slot * 1024 + ncols)
        b.wait_tok("sync", wslot["tok"][slot])
        if wslot["split"][slot] is not None:
            b.wait_tok("sync", wslot["split"][slot][0])
            b.wait_tok("sync", wslot["split"][slot][1])
        dst = wstage[:, sl]
        if dram_ap.ndim == 3:
            dst = dst.rearrange("p (f o) -> p f o", f=dram_ap.shape[1])
        t_d = b.dma("sync", dst, dram_ap, "dw%d" % slot)
        b.wait_tok("vector", t_d)
        b.wait_tok("vector", wslot["tok"][slot])
        b.wait_tok("gpsimd", t_d)
        b.wait_tok("gpsimd", wslot["tok"][slot])
        hs = wbh[:, sl]
        ls = wbl[:, sl]
        t_v = b.op("vector", lambda v, sl=sl, hs=hs: v.tensor_copy(hs, wstage[:, sl]), sem="va", nd=True)
        if lo_eng == "gpsimd":
            b.wait_tok("gpsimd", t_v)
            t_g = b.op("gpsimd", lambda g, sl=sl, hs=hs, ls=ls: g.tensor_tensor(
                out=ls, in0=wstage[:, sl], in1=hs, op=OP.subtract), sem="ga", nd=True)
        else:
            t_g = b.op("vector", lambda v, sl=sl, hs=hs, ls=ls: v.tensor_tensor(
                out=ls, in0=wstage[:, sl], in1=hs, op=OP.subtract), sem="va")
        wslot["split"][slot] = (t_v, t_g)
        return hs, ls, t_v, t_g

    def wslot_consumed(slot, tok):
        wslot["tok"][slot] = tok

    def rope_pair(pst, dst, tsl, sem):
        b.op("vector", lambda v, pst=pst, tsl=tsl: v.tensor_tensor(
            out=ev[:, 0:512], in0=pst, in1=cosqt[:, tsl], op=OP.mult))
        b.op("vector", lambda v, pst=pst, dst=dst, tsl=tsl: v.tensor_tensor(
            out=dst[0:64, :], in0=pst[64:128, :], in1=sinmt[0:64, tsl], op=OP.mult))
        b.op("vector", lambda v, pst=pst, dst=dst, tsl=tsl: v.tensor_tensor(
            out=dst[64:128, :], in0=pst[0:64, :], in1=sinmt[64:128, tsl], op=OP.mult))
        return b.op("vector", lambda v, dst=dst: v.tensor_tensor(
            out=dst, in0=dst, in1=ev[:, 0:512], op=OP.add), sem=sem)

    qkv_state = {"ps": [None, None, None, None]}
    qkv_banks = [psS1[:, 0:512], psS1[:, 512:1024], psS0[:, 0:512], psS0[:, 512:1024]]

    def qkv(w_ap, n_oc, evac, lo_eng="gpsimd"):
        prev_ps_tok = qkv_state["ps"]
        nhalf = 2 * n_oc
        fetched = {}

        def fetch(i):
            if i >= nhalf or i in fetched:
                return
            oc, hf = divmod(i, 2)
            fetched[i] = (weight_half(
                w_ap[oc, hf].rearrange("f p o -> p f o"), 1024, i % 3,
                lo_eng=lo_eng), i % 3)

        for i in range(3):
            fetch(i)
        for oc in range(n_oc):
            halves = [fetched[2 * oc], fetched[2 * oc + 1]]
            for th in range(2):
                pslot = (oc * 2 + th) % 4
                pst = qkv_banks[pslot]
                b.wait_tok("tensor", prev_ps_tok[pslot])
                t_mm = None
                for fc in range(FCH):
                    hf, fcl = divmod(fc, 8)
                    (hs, ls, t_v, t_g), slot = halves[hf]
                    if fcl == 0:
                        b.wait_tok("tensor", t_v)
                        b.wait_tok("tensor", t_g)
                    wh_ = hs[:, fcl * 128:(fcl + 1) * 128]
                    wl_ = ls[:, fcl * 128:(fcl + 1) * 128]
                    rh_ = lnh_of(Bbf, fc, th * 512, th * 512 + 512)
                    rl_ = lnl_of(Bbf, fc, th * 512, th * 512 + 512)
                    t_mm = b.mm3(pst, wh_, wl_, rh_, rl_, start=(fc == 0),
                                 stop=(fc == FCH - 1), sem="tq" if fc == FCH - 1 else None)
                prev_ps_tok[pslot] = evac(oc, th, pst, t_mm)
            wslot_consumed(fetched[2 * oc][1], t_mm)
            wslot_consumed(fetched[2 * oc + 1][1], t_mm)
            fetch(2 * oc + 3)
            fetch(2 * oc + 4)

    def evac_q(oc, th, pst, t_mm):
        tsl = slice(th * 512, (th + 1) * 512)
        dst = QC[:, oc * 1024 + th * 512: oc * 1024 + (th + 1) * 512]
        b.wait_tok("vector", t_mm)
        return rope_pair(pst, dst, tsl, "va")

    def evac_k(oc, th, pst, t_mm):
        j = oc
        tsl = slice(th * 512, (th + 1) * 512)
        f_ = ev[:, 512:1024]
        b.wait_tok("vector", t_mm)
        t_r = rope_pair(pst, f_, tsl, "va")
        kh = kvstage[:, 0:512]
        kl = kvstage[:, 512:1024]
        b.wait_tok("scalar", t_r)
        b.wait_tok("scalar", b._kv_tok)
        t_h = b.op("scalar", lambda s, kh=kh, f_=f_: s.activation(out=kh, in_=f_, func=AF.Copy), sem="sa")
        b.wait_tok("vector", t_h)
        t_l = b.op("vector", lambda v, kl=kl, f_=f_, kh=kh: v.tensor_tensor(
            out=kl, in0=f_, in1=kh, op=OP.subtract), sem="va")
        t_s = b.op("vector", lambda v, f_=f_: v.tensor_tensor(
            out=ev[:, 0:512], in0=f_, in1=f_, op=OP.mult), sem="va")
        b.wait_tok("tensor", t_s)
        t_n = b.op("tensor", lambda t: t.matmul(psX[0:1, :], onesf[:, 0:1], ev[:, 0:512], start=True, stop=True), sem="tt")
        b.wait_tok("vector", t_n)
        b.op("vector", lambda v: v.reduce_max(sc1[0:1, 0:1], psX[0:1, :], axis=AX.X))
        if th == 0:
            b.op("vector", lambda v, j=j: v.tensor_copy(knorm2[:, j:j + 1], sc1[0:1, 0:1]), sem="va")
        else:
            b.op("vector", lambda v, j=j: v.tensor_tensor(
                out=knorm2[:, j:j + 1], in0=knorm2[:, j:j + 1], in1=sc1[0:1, 0:1], op=OP.max), sem="va")
        b.wait_tok("sync", t_l)
        w0 = th * 256
        b.dma("sync", kpay[0, j, :, w0:w0 + 256], kh.bitcast(F32), "d")
        tok = b.dma("sync", kpay[1, j, :, w0:w0 + 256], kl.bitcast(F32), "d")
        b._kv_tok = tok
        return t_r

    def evac_v(oc, th, pst, t_mm):
        j = oc
        f_ = ev[:, 0:512]
        b.wait_tok("vector", t_mm)
        t_cp = b.op("vector", lambda v, f_=f_, pst=pst: v.tensor_copy(f_, pst), sem="va")
        tok = None
        prev = [None, None]
        for sub in range(4):
            tc = th * 4 + sub
            ptr = psC[:, (sub % 2) * 512:(sub % 2) * 512 + 128]
            b.wait_tok("tensor", t_cp)
            b.wait_tok("tensor", prev[sub % 2])
            t_tr = b.op("tensor", lambda t, sub=sub, ptr=ptr, f_=f_: t.transpose(
                ptr, f_[:, sub * 128:(sub + 1) * 128], eyet[:]), sem="tt")
            vh = kvstage[:, 0:128]
            vl = kvstage[:, 128:256]
            b.wait_tok("scalar", t_tr)
            b.wait_tok("scalar", b._kv_tok)
            t_h = b.op("scalar", lambda s, ptr=ptr, vh=vh: s.activation(out=vh, in_=ptr, func=AF.Copy), sem="sa")
            b.wait_tok("vector", t_h)
            t_l = b.op("vector", lambda v, ptr=ptr, vh=vh, vl=vl: v.tensor_tensor(
                out=vl, in0=ptr, in1=vh, op=OP.subtract), sem="va")
            prev[sub % 2] = t_l
            b.wait_tok("sync", t_l)
            b.dma("sync", vpay[0, j, tc], vh.bitcast(F32), "d")
            tok = b.dma("sync", vpay[1, j, tc], vl.bitcast(F32), "d")
            b._kv_tok = tok
        return t_cp

    b.wait("tensor", "vb")
    qkv(wk, NKV, evac_k)
    qkv(wv, NKV, evac_v)

    b.wait("sync", "va")
    b.dma("sync", npay.rearrange("(a j) -> a j", a=1), knorm2[:], "d")
    b.dump("pay", pay_in)

    b.wait("gpsimd", "d")
    rec = b.sems["cc"]; rec[1] += 1
    b.on("gpsimd", lambda g, h=rec[0]: g.collective_compute(
        "AllGather", OP.bypass, replica_groups=[list(range(NCORES))],
        ins=[pay_in.opt()], outs=[pay_out.opt()]).then_inc(h, 1))

    qkv(wq, NH, evac_q, lo_eng="vector")
    b.dump("qc", QC[:])

    if stage >= 2:
        for h in range(NH):
            qsl = QC[:, h * 1024:(h + 1) * 1024]
            t_q = b.op("vector", lambda v, qsl=qsl: v.tensor_tensor(
                out=qsq[:], in0=qsl, in1=qsl, op=OP.mult), sem="va")
            b.wait_tok("tensor", t_q)
            b.op("tensor", lambda t: t.matmul(psR[0:1, :], onesf[:, 0:1], qsq[:, 0:512], start=True, stop=True))
            t_n = b.op("tensor", lambda t: t.matmul(psX[0:1, :], onesf[:, 0:1], qsq[:, 512:1024], start=True, stop=True), sem="tt")
            b.wait_tok("vector", t_n)
            b.op("vector", lambda v: v.reduce_max(sc1[0:1, 0:1], psR[0:1, :], axis=AX.X))
            b.op("vector", lambda v: v.reduce_max(sc1[0:1, 1:2], psX[0:1, :], axis=AX.X))
            b.op("vector", lambda v: v.tensor_tensor(out=sc1[0:1, 0:1], in0=sc1[0:1, 0:1], in1=sc1[0:1, 1:2], op=OP.max))
            b.op("vector", lambda v, h=h: v.tensor_copy(qmax2[:, h:h + 1], sc1[0:1, 0:1]), sem="va")

    b.wait("gpsimd", "cc")
    t_gg = b.op("gpsimd", lambda g: g.memset(sc2[0:1, 0:1], 0.0), sem="ga")
    for e_ in ("sync", "tensor", "vector", "scalar"):
        b.wait_tok(e_, t_gg)
    b.dump("gat", pay_out)

    if stage >= 2:
        t_kn = b.dma("sync", kn_all[:].rearrange("a (r j) -> a r j", r=NCORES),
                     pay_out[:, 2 * KW + 2 * VW: 2 * KW + 2 * VW + NKV], "dn")
        b.wait_tok("vector", t_kn)
        b.op("vector", lambda v: v.reduce_max(
            kmax2[:], kn_all[:].rearrange("a (r j) -> a j r", r=NCORES), axis=AX.X), sem="va")
        for h in range(NH):
            b.op("vector", lambda v, h=h: v.tensor_tensor(
                out=sc2[0:1, h:h + 1], in0=qmax2[:, h:h + 1],
                in1=kmax2[:, h // 2:h // 2 + 1], op=OP.mult),
                sem="va" if h == NH - 1 else None)
        b.wait("scalar", "va")
        t_sq2 = b.op("scalar", lambda s: s.activation(out=sc1[0:1, 0:NH], in_=sc2[0:1, 0:NH], func=AF.Sqrt), sem="sa")
        b.wait_tok("vector", t_sq2)
        t_ng = b.op("vector", lambda v: v.tensor_scalar(
            out=sc1[0:1, NH:2 * NH], in0=sc1[0:1, 0:NH], scalar1=-INV_SQRT_E,
            scalar2=None, op0=OP.mult), sem="va")
        b.wait_tok("tensor", t_ng)
        t_bc = b.op("tensor", lambda t: t.matmul(psR[:, 0:NH], onesf[0:1, :], sc1[0:1, NH:2 * NH], start=True, stop=True), sem="tt")
        b.wait_tok("vector", t_bc)
        b.op("vector", lambda v: v.tensor_copy(mh[:], psR[:, 0:NH]), sem="va")
        b.wait("scalar", "va")

        kvbuf = [kvb0, kvb1]
        flat = pay_out.rearrange("r n -> (r n)")

        def kv_dma(j, blk, tok_free):
            r = blk
            buf = kvbuf[blk % 2]
            b.wait_tok("sync", tok_free)
            base_kh = r * PAY + j * 128 * 512
            base_kl = r * PAY + KW + j * 128 * 512
            base_vh = r * PAY + 2 * KW + j * TCH * 128 * 64
            base_vl = r * PAY + 2 * KW + VW + j * TCH * 128 * 64
            ds = "dkv%d" % (blk % 2)
            b.dma("sync", buf[:, 0:1024].bitcast(F32),
                  flat[base_kh: base_kh + 128 * 512].rearrange("(p w) -> p w", p=128), ds)
            b.dma("sync", buf[:, 1024:2048].bitcast(F32),
                  flat[base_kl: base_kl + 128 * 512].rearrange("(p w) -> p w", p=128), ds)
            b.dma("sync", buf[:, 2048:3072].bitcast(F32).rearrange("p (c w) -> p c w", c=TCH),
                  flat[base_vh: base_vh + TCH * 128 * 64].rearrange("(c p w) -> p c w", c=TCH, p=128), ds)
            return b.dma("sync", buf[:, 3072:4096].bitcast(F32).rearrange("p (c w) -> p c w", c=TCH),
                         flat[base_vl: base_vl + TCH * 128 * 64].rearrange("(c p w) -> p c w", c=TCH, p=128), ds)

        NU = 2 * KCH
        psSslot = [psS0[:, 0:512], psS0[:, 512:1024], psS1[:, 0:512], psS1[:, 512:1024]]
        prev_sweep_pv = ("tq", b.sems["tq"][1])
        prev_sweep_scores = ("tq", b.sems["tq"][1])
        ctx_ch_tok = [None]
        for j in range(NKV):
            for hh in range(2):
                h = 2 * j + hh
                b.wait("tensor", "se")
                b.wait("tensor", "sh", b.sems["sh"][1])
                b.wait("scalar", "vh")
                b.wait("scalar", "vr", b.sems["vr"][1])
                b.wait("scalar", "tp", b.sems["tp"][1])
                b.wait("vector", "tp")
                b.wait("vector", "sh", b.sems["sh"][1])
                qsrc = QC[:, h * 1024:(h + 1) * 1024]
                b.wait_tok("vector", prev_sweep_scores)
                b.wait_tok("vector", ctx_ch_tok[0])
                t_z0 = b.op("vector", lambda v: v.memset(ctmp[:], 0.0), sem="vr")
                acc_tok = [t_z0, t_z0]
                t_qh = b.op("vector", lambda v, qsrc=qsrc: v.tensor_copy(qh_s, qsrc), sem="vh", nd=True)
                b.wait_tok("gpsimd", t_qh)
                t_ql = b.op("gpsimd", lambda g, qsrc=qsrc: g.tensor_tensor(
                    out=ql_s, in0=qsrc, in1=qh_s, op=OP.subtract), sem="gl", nd=True)
                b.wait_tok("tensor", t_qh)
                b.wait_tok("tensor", t_ql)
                e_tok = [None] * NU
                h_tok = [None] * NU
                l_tok = [None] * NU
                s_tok = [None] * NU
                p_tok = [None] * NU
                a_tok = [None] * NU
                d_tok = [None] * 8
                d_tok[0] = kv_dma(j, 0, prev_sweep_pv)
                d_tok[1] = kv_dma(j, 1, prev_sweep_pv)

                def kv_tiles(u):
                    c, qh2 = divmod(u, 2)
                    blk, lc = divmod(c, 8)
                    buf = kvbuf[blk % 2]
                    return (blk, qh2,
                            buf[:, 0:1024][:, lc * 128:(lc + 1) * 128],
                            buf[:, 1024:2048][:, lc * 128:(lc + 1) * 128],
                            buf[:, 2048:3072][:, lc * 128:(lc + 1) * 128],
                            buf[:, 3072:4096][:, lc * 128:(lc + 1) * 128])

                def emit_scores(u):
                    blk, qh2, Kh, Kl, Vh, Vl = kv_tiles(u)
                    b.wait_tok("tensor", d_tok[blk])
                    if u >= 4:
                        b.wait_tok("tensor", e_tok[u - 4])
                    s_tok[u] = b.mm3(psSslot[u % 4], Kh, Kl,
                                     qh_s[:, qh2 * 512:(qh2 + 1) * 512],
                                     ql_s[:, qh2 * 512:(qh2 + 1) * 512],
                                     start=True, stop=True, sem="ts")

                for u0 in range(3):
                    emit_scores(u0)
                for u in range(NU):
                    c, qh2 = divmod(u, 2)
                    blk, _, Kh, Kl, Vh, Vl = kv_tiles(u)
                    sl = u % 4
                    pb = pbuf[:, sl * 512:(sl + 1) * 512]
                    phb = phv[:, sl * 512:(sl + 1) * 512]
                    plb = pl_[:, sl * 512:(sl + 1) * 512]
                    b.wait_tok("scalar", s_tok[u])
                    if u >= 4:
                        b.wait_tok("scalar", l_tok[u - 4])
                        b.wait_tok("scalar", a_tok[u - 4])
                    e_tok[u] = b.op("scalar", lambda s, ps_=psSslot[sl], pb=pb, h=h: s.activation(
                        out=pb, in_=ps_, func=AF.Exp, bias=mh[:, h:h + 1],
                        scale=INV_SQRT_E), sem="se", nd=True)
                    b.wait_tok("scalar", e_tok[u])
                    if u >= 4:
                        b.wait_tok("scalar", p_tok[u - 4])
                        b.wait_tok("scalar", l_tok[u - 4])
                    h_tok[u] = b.op("scalar", lambda s, pb=pb, phb=phb: s.activation(
                        out=phb, in_=pb, func=AF.Copy), sem="sh", nd=True)
                    b.wait_tok("vector", h_tok[u])
                    if u >= 4:
                        b.wait_tok("vector", p_tok[u - 4])
                    l_tok[u] = b.op("vector", lambda v, pb=pb, phb=phb, plb=plb: v.tensor_tensor(
                        out=plb, in0=pb, in1=phb, op=OP.subtract), sem="vh", nd=True)
                    b.wait_tok("vector", acc_tok[qh2])
                    acc_tok[qh2] = b.op("vector", lambda v, pb=pb, qh2=qh2: v.tensor_tensor(
                        out=ctmp[:, qh2 * 512:(qh2 + 1) * 512],
                        in0=ctmp[:, qh2 * 512:(qh2 + 1) * 512], in1=pb, op=OP.add),
                        sem="vr", nd=True)
                    a_tok[u] = acc_tok[qh2]
                    b.wait_tok("tensor", h_tok[u])
                    b.wait_tok("tensor", l_tok[u])
                    first, last = (c == 0), (c == KCH - 1)
                    p_tok[u] = b.mm3(psC[:, qh2 * 512:(qh2 + 1) * 512], Vh, Vl, phb, plb,
                                     start=first, stop=last, sem="tp")
                    if qh2 == 1 and c % 8 == 7 and blk + 2 < 8:
                        d_tok[blk + 2] = kv_dma(j, blk + 2, p_tok[u])
                    if u + 3 < NU:
                        emit_scores(u + 3)
                prev_sweep_pv = p_tok[NU - 1]
                prev_sweep_scores = s_tok[NU - 1]
                b.wait_tok("tensor", acc_tok[0])
                b.wait_tok("tensor", acc_tok[1])
                b.op("tensor", lambda t: t.matmul(psR[0:1, :], onesf[:, 0:1], ctmp[:, 0:512], start=True, stop=True))
                t_rsum = b.op("tensor", lambda t: t.matmul(psX[0:1, :], onesf[:, 0:1], ctmp[:, 512:1024], start=True, stop=True), sem="tt")
                b.wait_tok("vector", t_rsum)
                b.wait_tok("vector", p_tok[NU - 1])
                b.op("vector", lambda v: v.reciprocal(ev[0:1, 0:512], psR[0:1, :]))
                t_rc = b.op("vector", lambda v: v.reciprocal(ev[0:1, 512:1024], psX[0:1, :]), sem="va")
                b.wait_tok("tensor", t_rc)
                t_b0 = b.op("tensor", lambda t: t.matmul(psX[:, :], onesf[0:1, :], ev[0:1, 0:512], start=True, stop=True), sem="tt")
                b.wait_tok("vector", t_b0)
                t_c0 = b.op("vector", lambda v: v.tensor_copy(inv_rs[:, 0:512], psX[:]), sem="va")
                b.wait_tok("tensor", t_c0)
                t_b1 = b.op("tensor", lambda t: t.matmul(psX[:, :], onesf[0:1, :], ev[0:1, 512:1024], start=True, stop=True), sem="tt")
                b.wait_tok("vector", t_b1)
                b.op("vector", lambda v: v.tensor_copy(inv_rs[:, 512:1024], psX[:]), sem="va")
                t_nm = b.op("vector", lambda v: v.tensor_tensor(out=ctmp[:], in0=psC[:], in1=inv_rs[:], op=OP.mult), sem="va")
                ch = Abf[:, h * 2048: h * 2048 + 1024]
                cl = Abf[:, h * 2048 + 1024: h * 2048 + 2048]
                b.wait_tok("scalar", t_nm)
                t_ch = b.op("scalar", lambda s, ch=ch: s.activation(out=ch, in_=ctmp[:], func=AF.Copy), sem="sa", nd=True)
                ctx_ch_tok[0] = t_ch
                b.wait_tok("vector", t_ch)
                b.op("vector", lambda v, ch=ch, cl=cl: v.tensor_tensor(
                    out=cl, in0=ctmp[:], in1=ch, op=OP.subtract), sem="vb", nd=True)

        b.dump("ctx", A[:])

    if stage >= 3:
        prev_ev_tok = [None] * TCH
        prev_x_tok = None
        wo_f = {}

        def wo_fetch(t):
            if t >= 64 or t in wo_f:
                return
            wo_f[t] = (weight_half(wo[t // FCH, t % FCH], 512, t % 3), t % 3)

        wo_fetch(0); wo_fetch(1); wo_fetch(2)
        for oh in range(4):
            for fc in range(FCH):
                tno = oh * FCH + fc
                (hs, ls, t_v, t_g), slot = wo_f[tno]
                b.wait_tok("tensor", t_v)
                b.wait_tok("tensor", t_g)
                t_mm = None
                for qc in range(TCH):
                    ch = Abf[:, fc * 2048 + qc * 128: fc * 2048 + qc * 128 + 128]
                    cl = Abf[:, fc * 2048 + 1024 + qc * 128: fc * 2048 + 1024 + qc * 128 + 128]
                    if fc == 0:
                        b.wait_tok("tensor", prev_ev_tok[qc])
                    t_mm = b.mm3(bank[qc], ch, cl, hs[:, 0:512], ls[:, 0:512],
                                 start=(fc == 0), stop=(fc == FCH - 1),
                                 sem="tq" if qc == TCH - 1 else None)
                wslot_consumed(slot, t_mm)
                wo_fetch(tno + 3)
            for qc in range(TCH):
                b.wait_tok("sync", prev_x_tok)
                t_x = b.dma("sync", ev[:, 0:512],
                            x_in[qc * 128:(qc + 1) * 128, oh * 512:(oh + 1) * 512], "d")
                b.wait_tok("vector", t_x)
                b.wait("vector", "tq")
                prev_ev_tok[qc] = b.op("vector", lambda v, qc=qc, oh=oh: v.tensor_tensor(
                    out=x2[:, qc * 2048 + oh * 512: qc * 2048 + (oh + 1) * 512],
                    in0=bank[qc], in1=ev[:, 0:512], op=OP.add), sem="vb", nd=True)
                prev_x_tok = prev_ev_tok[qc]
        b.dump("x2", B[:])

    if stage >= 4:
        for tc in range(TCH):
            src = x2[:, tc * 2048:(tc + 1) * 2048]
            b.wait("vector", "vb")
            b.wait("scalar", "vb", b.sems["vb"][1])
            t_ap = emit_ln(src)
            emit_transpose_splits(tc, Abf, t_ap)

        hs1, ls1, t_v, t_g = weight_half(
            rw1.rearrange("f p r -> p f r"), FCH * RHID, 0)
        b.wait("tensor", "vb")
        b.wait_tok("tensor", t_v)
        b.wait_tok("tensor", t_g)
        for th in range(2):
            pst = psR if th == 0 else psX
            t_mm = None
            for fc in range(FCH):
                wh_ = hs1[:, fc * RHID:(fc + 1) * RHID]
                wl_ = ls1[:, fc * RHID:(fc + 1) * RHID]
                rh_ = lnh_of(Abf, fc, th * 512, th * 512 + 512)
                rl_ = lnl_of(Abf, fc, th * 512, th * 512 + 512)
                t_mm = b.mm3(pst[0:RHID, :], wh_, wl_, rh_, rl_, start=(fc == 0),
                             stop=(fc == FCH - 1), sem="tq" if fc == FCH - 1 else None)
            wslot_consumed(0, t_mm)
            b.wait_tok("scalar", t_mm)
            b.op("scalar", lambda s, pst=pst, th=th: s.activation(
                out=h_fm[:, th * 512:(th + 1) * 512], in_=pst[0:RHID, :],
                func=AF.Silu, bias=rb1t[:], scale=1.0), sem="sa")
        b.wait("vector", "sa")
        t_hh = b.op("vector", lambda v: v.tensor_copy(h_h[:], h_fm[:]), sem="va")
        b.wait_tok("gpsimd", t_hh)
        t_hl = b.op("gpsimd", lambda g: g.tensor_tensor(out=h_l[:], in0=h_fm[:], in1=h_h[:], op=OP.subtract), sem="ga")
        b.wait_tok("sync", wslot["tok"][1])
        if wslot["split"][1] is not None:
            b.wait_tok("sync", wslot["split"][1][0])
            b.wait_tok("sync", wslot["split"][1][1])
        t_w2 = b.dma("sync", wstage[0:RHID, 1024:1056], rw2, "dw1")
        b.wait_tok("vector", t_w2)
        t_r2h = b.op("vector", lambda v: v.tensor_copy(r2h[:], wstage[0:RHID, 1024:1056]), sem="va")
        b.wait_tok("gpsimd", t_r2h)
        t_r2l = b.op("gpsimd", lambda g: g.tensor_tensor(out=r2l[:], in0=wstage[0:RHID, 1024:1056], in1=r2h[:], op=OP.subtract), sem="ga")
        wslot["split"][1] = (t_r2h, t_r2l)
        for tk in (t_hh, t_r2h):
            b.wait_tok("tensor", tk)
        for tk in (t_hl, t_r2l):
            b.wait_tok("tensor", tk)
        for th in range(2):
            pst = psR if th == 0 else psX
            t_mm = b.mm3(pst[0:32, :], r2h[:], r2l[:],
                         h_h[:, th * 512:(th + 1) * 512],
                         h_l[:, th * 512:(th + 1) * 512],
                         start=True, stop=True, sem="tq")
            b.wait_tok("vector", t_mm)
            b.op("vector", lambda v, pst=pst, th=th: v.tensor_scalar(
                out=lg_fm[:, th * 512:(th + 1) * 512], in0=pst[0:16, :],
                scalar1=rbgt[:], scalar2=None, op0=OP.add), sem="va")
        b.wait("tensor", "va")
        prev_cf = None
        for tc in range(TCH):
            t_tr = b.op("tensor", lambda t, tc=tc: t.transpose(
                psR[:, 0:16], lg_fm[:, tc * 128:(tc + 1) * 128], eyet[0:16, 0:16]), sem="tt")
            b.wait_tok("vector", t_tr)
            b.wait_tok("vector", prev_cf)
            b.op("vector", lambda v: v.tensor_copy(lg_tm[:], psR[:, 0:16]))
            b.op("vector", lambda v: v.reduce_max(sc1[:, 0:1], lg_tm[:], axis=AX.X))
            b.op("vector", lambda v: v.tensor_scalar(out=ind1[:], in0=lg_tm[:],
                 scalar1=sc1[:, 0:1], scalar2=None, op0=OP.is_equal))
            b.op("vector", lambda v: v.scalar_tensor_tensor(out=msk1[:], in0=ind1[:],
                 scalar=-1e30, in1=lg_tm[:], op0=OP.mult, op1=OP.add))
            b.op("vector", lambda v: v.reduce_max(sc1[:, 1:2], msk1[:], axis=AX.X))
            b.op("vector", lambda v: v.tensor_scalar(out=msk2[:], in0=msk1[:],
                 scalar1=sc1[:, 1:2], scalar2=None, op0=OP.is_equal))
            t_d1 = b.op("vector", lambda v: v.tensor_tensor(out=sc1[:, 2:3], in0=sc1[:, 1:2], in1=sc1[:, 0:1], op=OP.subtract), sem="va")
            b.wait_tok("scalar", t_d1)
            t_z = b.op("scalar", lambda s: s.activation(out=sc1[:, 3:4], in_=sc1[:, 2:3], func=AF.Exp), sem="sa")
            b.wait_tok("vector", t_z)
            b.op("vector", lambda v: v.tensor_scalar(out=sc1[:, 4:5], in0=sc1[:, 3:4], scalar1=1.0, scalar2=None, op0=OP.add))
            b.op("vector", lambda v: v.reciprocal(sc1[:, 5:6], sc1[:, 4:5]))
            b.op("vector", lambda v: v.tensor_tensor(out=sc1[:, 6:7], in0=sc1[:, 3:4], in1=sc1[:, 5:6], op=OP.mult))
            b.op("vector", lambda v: v.tensor_scalar(out=cf_tm[:], in0=ind1[:], scalar1=sc1[:, 5:6], scalar2=None, op0=OP.mult))
            t_cf = b.op("vector", lambda v: v.scalar_tensor_tensor(
                out=cf_tm[:], in0=msk2[:], scalar=sc1[:, 6:7], in1=cf_tm[:],
                op0=OP.mult, op1=OP.add), sem="va")
            b.wait_tok("tensor", t_cf)
            t_tr2 = b.op("tensor", lambda t: t.transpose(psX[0:16, 0:128], cf_tm[:], eyet[:]), sem="tt")
            b.wait_tok("vector", t_tr2)
            prev_cf = b.op("vector", lambda v, tc=tc: v.tensor_copy(
                cf_fm[:, tc * 128:(tc + 1) * 128], psX[0:16, 0:128]), sem="va")
        b.dump("cf", cf_fm[:])

        b.wait_tok("tensor", prev_cf)
        for ec in range(2):
            for th in range(2):
                pst = psR if th == 0 else psX
                t_mm = b.op("tensor", lambda t, ec=ec, th=th, pst=pst: t.matmul(
                    pst[:, :], ematt[:, ec * 128:(ec + 1) * 128],
                    cf_fm[:, th * 512:(th + 1) * 512], start=True, stop=True), sem="tt")
                b.wait_tok("vector", t_mm)
                b.op("vector", lambda v, th=th, pst=pst: v.tensor_copy(
                    cfe[:, th * 512:(th + 1) * 512], pst[:, :]), sem="va")
            halves = []
            for hf in range(2):
                slot = hf
                hs, ls, t_v, t_g = weight_half(
                    wa[ec, hf].rearrange("f p o -> p f o"), 1024, slot)
                halves.append((hs, ls, t_v, t_g))
            for th in range(2):
                pst = psS0[:, th * 512:(th + 1) * 512] if ec == 0 else psS1[:, th * 512:(th + 1) * 512]
                t_mm = None
                for fc in range(FCH):
                    hf, fcl = divmod(fc, 8)
                    hs, ls, t_v, t_g = halves[hf]
                    if fcl == 0 and th == 0:
                        b.wait_tok("tensor", t_v)
                        b.wait_tok("tensor", t_g)
                    wh_ = hs[:, fcl * 128:(fcl + 1) * 128]
                    wl_ = ls[:, fcl * 128:(fcl + 1) * 128]
                    rh_ = lnh_of(Abf, fc, th * 512, th * 512 + 512)
                    rl_ = lnl_of(Abf, fc, th * 512, th * 512 + 512)
                    t_mm = b.mm3(pst, wh_, wl_, rh_, rl_, start=(fc == 0),
                                 stop=(fc == FCH - 1), sem="tq" if fc == FCH - 1 else None)
                wslot_consumed(0, t_mm)
                wslot_consumed(1, t_mm)
                b.wait_tok("vector", t_mm)
                b.wait("vector", "va")
                t_xa = b.op("vector", lambda v, pst=pst, th=th: v.tensor_tensor(
                    out=xa_t[:, th * 512:(th + 1) * 512], in0=pst,
                    in1=cfe[:, th * 512:(th + 1) * 512], op=OP.mult), sem="va")
                b.wait_tok("scalar", t_xa)
                t_sh = b.op("scalar", lambda s, ec=ec, th=th: s.activation(
                    out=wxh[:, ec * 1024 + th * 512: ec * 1024 + (th + 1) * 512],
                    in_=xa_t[:, th * 512:(th + 1) * 512], func=AF.Copy), sem="sa")
                b.wait_tok("vector", t_sh)
                b.op("vector", lambda v, ec=ec, th=th: v.tensor_tensor(
                    out=wxl[:, ec * 1024 + th * 512: ec * 1024 + (th + 1) * 512],
                    in0=xa_t[:, th * 512:(th + 1) * 512],
                    in1=wxh[:, ec * 1024 + th * 512: ec * 1024 + (th + 1) * 512],
                    op=OP.subtract), sem="vb")

        b.wait("tensor", "vb")
        prev_ev_tok = [None] * TCH
        prev_out_tok = [None, None]
        ff_f = {}

        def ff_fetch(t):
            if t >= 72 or t in ff_f:
                return
            ohx, u = divmod(t, 18)
            ap = wff[ohx, u] if u < FCH else wb[ohx, u - FCH]
            ff_f[t] = (weight_half(ap, 512, t % 3), t % 3)

        ff_fetch(0); ff_fetch(1); ff_fetch(2)
        for oh in range(4):
            t_fb = b.dma("sync", fbias[:], ffb[oh:oh + 1, :], "dfb")
            for fc in range(FCH):
                tno = oh * 18 + fc
                (hs, ls, t_v, t_g), slot = ff_f[tno]
                b.wait_tok("tensor", t_v)
                b.wait_tok("tensor", t_g)
                t_mm = None
                for qc in range(TCH):
                    lh = lnh_of(Abf, fc, qc * 128, qc * 128 + 128)
                    ll = lnl_of(Abf, fc, qc * 128, qc * 128 + 128)
                    if fc == 0:
                        b.wait_tok("tensor", prev_ev_tok[qc])
                    t_mm = b.mm3(bank[qc], lh, ll, hs[:, 0:512], ls[:, 0:512],
                                 start=(fc == 0), stop=False,
                                 sem="tq" if qc == TCH - 1 else None)
                wslot_consumed(slot, t_mm)
                ff_fetch(tno + 3)
            for ec in range(2):
                tno = oh * 18 + FCH + ec
                (hs, ls, t_v, t_g), slot = ff_f[tno]
                b.wait_tok("tensor", t_v)
                b.wait_tok("tensor", t_g)
                t_mm = None
                for qc in range(TCH):
                    xh_ = wxh[:, ec * 1024 + qc * 128: ec * 1024 + qc * 128 + 128]
                    xl_ = wxl[:, ec * 1024 + qc * 128: ec * 1024 + qc * 128 + 128]
                    t_mm = b.mm3(bank[qc], xh_, xl_, hs[:, 0:512], ls[:, 0:512],
                                 start=False, stop=False,
                                 sem="tq" if qc == TCH - 1 else None)
                wslot_consumed(slot, t_mm)
                ff_fetch(tno + 3)
            b.wait_tok("tensor", t_fb)
            t_last = [None] * TCH
            for qc in range(TCH):
                t_last[qc] = b.op("tensor", lambda t, qc=qc: t.matmul(
                    bank[qc], onesf[0:1, :], fbias[:], start=False, stop=True), sem="tq")
            for qc in range(TCH):
                oslot = qc % 2
                b.wait_tok("vector", t_last[qc])
                b.wait_tok("vector", prev_out_tok[oslot])
                t_ev = b.op("vector", lambda v, qc=qc, oh=oh, oslot=oslot: v.tensor_tensor(
                    out=outt[:, oslot * 512:(oslot + 1) * 512],
                    in0=bank[qc],
                    in1=x2[:, qc * 2048 + oh * 512: qc * 2048 + (oh + 1) * 512],
                    op=OP.add), sem="vb", nd=True)
                prev_ev_tok[qc] = t_ev
                b.wait_tok("sync", t_ev)
                prev_out_tok[oslot] = b.dma(
                    "sync", y_out[qc * 128:(qc + 1) * 128, oh * 512:(oh + 1) * 512],
                    outt[:, oslot * 512:(oslot + 1) * 512], "dout%d" % oslot)

    if stage < 4:
        t_z = b.op("vector", lambda v: v.memset(ev[:, 0:512], 0.0), sem="va")
        b.wait_tok("sync", t_z)
        for tc in range(TCH):
            for oh in range(4):
                b.dma("sync", y_out[tc * 128:(tc + 1) * 128, oh * 512:(oh + 1) * 512],
                      ev[:, 0:512], "dout")

    b.wait("sync", "dout")
    b.wait("sync", "dout0")
    b.wait("sync", "dout1")
    _replay(b)
    return b


def _replay(b):
    blk = b.stack.enter_context(b.nc.Block())
    by_eng = {"sync": [], "tensor": [], "vector": [], "scalar": [], "gpsimd": []}
    for eng, fn in b.ops:
        by_eng[eng].append(fn)

    @blk.sync
    def _(e):
        for fn in by_eng["sync"]:
            fn(e)

    @blk.tensor
    def _(e):
        for fn in by_eng["tensor"]:
            fn(e)

    @blk.vector
    def _(e):
        for fn in by_eng["vector"]:
            fn(e)

    @blk.scalar
    def _(e):
        for fn in by_eng["scalar"]:
            fn(e)

    @blk.gpsimd
    def _(e):
        for fn in by_eng["gpsimd"]:
            fn(e)

    b.stack.close()



def _prep_weights(inp):
    f32 = np.float32
    WQ = np.asarray(inp["L0_attn_W_Q"], f32)
    WK = np.asarray(inp["L0_attn_W_K"], f32)
    WV = np.asarray(inp["L0_attn_W_V"], f32)
    WO = np.asarray(inp["L0_attn_W_O"], f32)
    permq = (np.arange(HD)[None, :] * NH + np.arange(NH)[:, None]).reshape(-1)
    permk = (np.arange(HD)[None, :] * NKV + np.arange(NKV)[:, None]).reshape(-1)

    def lhsT_tiles(Wp, noc):
        Wt = Wp.T
        t = Wt.reshape(2, 8, 128, noc, 128).transpose(3, 0, 1, 2, 4)
        return np.ascontiguousarray(t)

    wq_t = lhsT_tiles(WQ[permq], NH)
    wk_t = lhsT_tiles(WK[permk], NKV)
    wv_t = lhsT_tiles(WV[permk], NKV)
    wo_t = np.ascontiguousarray(WO.T.reshape(FCH, 128, 4, 512).transpose(2, 0, 1, 3))
    wff_t = np.ascontiguousarray(np.asarray(inp["L0_ffn_W"], f32).T
                                 .reshape(FCH, 128, 4, 512).transpose(2, 0, 1, 3))
    Amat = np.asarray(inp["L0_ffn_A"], f32).reshape(ER, E)
    wa_t = np.ascontiguousarray(Amat.T.reshape(2, 8, 128, 2, 128).transpose(3, 0, 1, 2, 4))
    Bmat = np.asarray(inp["L0_ffn_B"], f32).transpose(0, 2, 1).reshape(ER, E)
    wb_t = np.ascontiguousarray(Bmat.reshape(2, 128, 4, 512).transpose(2, 0, 1, 3))
    rw1_t = np.ascontiguousarray(np.asarray(inp["L0_ffn_rw1"], f32).T.reshape(FCH, 128, RHID))
    rw2_t = np.ascontiguousarray(np.asarray(inp["L0_ffn_rw2"], f32).T)
    ffb = np.ascontiguousarray(np.asarray(inp["L0_ffn_bias"], f32).reshape(4, 512))
    rb1 = np.ascontiguousarray(np.asarray(inp["L0_ffn_rb1"], f32).reshape(RHID, 1))
    rbg = np.ascontiguousarray((np.asarray(inp["L0_ffn_rb2"], f32)[:NEXP]
                                + np.asarray(inp["L0_ffn_gates"], f32)).reshape(NEXP, 1))
    emat = np.zeros((NEXP, ER), f32)
    for e_ in range(NEXP):
        emat[e_, e_ * RANK:(e_ + 1) * RANK] = 1.0
    return dict(wq=wq_t, wk=wk_t, wv=wv_t, wo=wo_t, wff=wff_t, wa=wa_t, wb=wb_t,
                rw1=rw1_t, rw2=rw2_t, ffb=ffb, rb1=rb1, rbg=rbg,
                eye=np.eye(128, dtype=f32), emat=emat)


def _rope_tables(core):
    d = np.arange(HD)
    invfreq = 10000.0 ** (-(d % 64).astype(np.float64) / 64.0)
    pos = np.arange(core * T, (core + 1) * T, dtype=np.float64)
    ang = invfreq[:, None] * pos[None, :]
    cos = np.cos(ang).astype(np.float32)
    sin = np.sin(ang).astype(np.float32)
    sinm = sin.copy()
    sinm[:64] = -sin[:64]
    return np.ascontiguousarray(cos), np.ascontiguousarray(sinm)


def make_in_maps(inputs):
    x = np.asarray(inputs["L0_attn_x"], np.float32).reshape(S, E)
    w = _prep_weights(inputs)
    in_maps = []
    for c in range(NCORES):
        cos, sinm = _rope_tables(c)
        m = dict(w)
        m["x"] = np.ascontiguousarray(x[c * T:(c + 1) * T])
        m["cosq"] = cos
        m["sinm"] = sinm
        in_maps.append(m)
    return in_maps


def kernel(**inputs):
    b = build()
    in_maps = make_in_maps(inputs)
    res = run_bass_kernel_spmd(b.nc, in_maps, core_ids=list(range(NCORES)))
    out = np.concatenate([r["y"] for r in res.results], axis=0)
    return out.reshape(1, S, E)


# version 08
# speedup vs baseline: 1.4237x; 1.0238x over previous
from contextlib import ExitStack

import numpy as np

import concourse.bass as bass
import concourse.mybir as mybir
from concourse.bass_utils import run_bass_kernel_spmd

F32 = mybir.dt.float32
BF16 = mybir.dt.bfloat16
AF = mybir.ActivationFunctionType
OP = mybir.AluOpType
AX = mybir.AxisListType

NCORES = 8
S = 8192
T = S // NCORES
E = 2048
HD = 128
NH = 16
NKV = 8
FCH = E // 128
TCH = T // 128
KCH = S // 128
RANK = 16
NEXP = 16
ER = RANK * NEXP
RHID = 64
INV_SQRT_E = float(np.float32(1.0 / np.sqrt(np.float64(E))))
LN_EPS = 1e-5

KW = NKV * 128 * (T // 2)
VW = NKV * TCH * 128 * (HD // 2)
PAY = 2 * KW + 2 * VW + NKV

STAGE = 4


class Builder:
    def __init__(self, stage=STAGE, debug=()):
        self.stage = stage
        self.debug = list(debug)
        self.nc = bass.Bass("TRN2", target_bir_lowering=False, debug=False)
        self.stack = ExitStack()
        self.ops = []
        self.sems = {}
        self.dbg = {}

    def sem(self, name):
        h = self.stack.enter_context(self.nc.semaphore(name))
        self.sems[name] = [h, 0]

    def sb(self, name, shape, dt=F32):
        return self.stack.enter_context(self.nc.sbuf_tensor(name, shape, dt))

    def ps(self, name, shape):
        return self.stack.enter_context(self.nc.psum_tensor(name, shape, F32))

    def on(self, eng, fn):
        self.ops.append((eng, fn))

    def wait(self, eng, name, value=None):
        h = self.sems[name][0]
        v = self.sems[name][1] if value is None else value
        if v > 0:
            self.on(eng, lambda e, h=h, v=v: e.wait_ge(h, v))

    def wait_tok(self, eng, tok):
        if tok is not None:
            self.wait(eng, tok[0], tok[1])

    DRAIN_ENGS = ("vector", "scalar", "gpsimd")

    def op(self, eng, fn, sem=None, n=1, nd=False):
        if eng in self.DRAIN_ENGS and not nd:
            self.on(eng, lambda e: e.drain())
        if sem is None:
            self.on(eng, fn)
            return None
        rec = self.sems[sem]
        rec[1] += n
        inc = (rec[0], n)
        self.on(eng, lambda e, fn=fn, inc=inc: fn(e).then_inc(*inc))
        return (sem, rec[1])

    def dma(self, eng, out, in_, sem):
        return self.op(eng, lambda e, out=out, in_=in_: e.dma_start(out=out, in_=in_),
                       sem=sem, n=16)

    def mm3(self, out, ah, al, bh, bl, start, stop, sem=None):
        self.on("tensor", lambda t, o=out, a=ah, b_=bh, s=start: t.matmul(o, a, b_, start=s, stop=False))
        self.on("tensor", lambda t, o=out, a=ah, b_=bl: t.matmul(o, a, b_, start=False, stop=False))
        return self.op("tensor", lambda t, o=out, a=al, b_=bh, p=stop: t.matmul(o, a, b_, start=False, stop=p), sem=sem)

    def dump(self, name, ap):
        if name not in self.debug:
            return
        d = self.nc.dram_tensor("dbg_" + name, list(ap.shape), ap.dtype,
                                kind="ExternalOutput").ap()
        self.dbg[name] = d
        for e_ in self.sems:
            if e_ not in ("dout",):
                self.wait("sync", e_)
        self.dma("sync", d, ap, "dout")


def build(stage=STAGE, debug=()):
    b = Builder(stage, debug)
    nc = b.nc

    din = lambda n, s: nc.dram_tensor(n, s, F32, kind="ExternalInput").ap()
    x_in = din("x", [T, E])
    wq = din("wq", [NH, 2, 8, 128, 128])
    wk = din("wk", [NKV, 2, 8, 128, 128])
    wv = din("wv", [NKV, 2, 8, 128, 128])
    wo = din("wo", [4, FCH, 128, 512])
    wff = din("wff", [4, FCH, 128, 512])
    wa = din("wa", [2, 2, 8, 128, 128])
    wb = din("wb", [4, 2, 128, 512])
    rw1 = din("rw1", [FCH, 128, RHID])
    rw2 = din("rw2", [RHID, 32])
    ffb = din("ffb", [4, 512])
    rb1 = din("rb1", [RHID, 1])
    rbg = din("rbg", [NEXP, 1])
    eye = din("eye", [128, 128])
    cosq = din("cosq", [128, T])
    sinm = din("sinm", [128, T])
    emat = din("emat", [16, ER])
    y_out = nc.dram_tensor("y", [T, E], F32, kind="ExternalOutput").ap()
    pay_in = nc.dram_tensor("pay_in", [PAY], F32).ap()
    pay_out = nc.dram_tensor("pay_out", [NCORES, PAY], F32, addr_space="Shared").ap()

    kpay = pay_in[0:2 * KW].rearrange("(s j p w) -> s j p w", s=2, j=NKV, p=128)
    vpay = pay_in[2 * KW:2 * KW + 2 * VW].rearrange("(s j c p w) -> s j c p w",
                                                    s=2, j=NKV, c=TCH, p=128)
    npay = pay_in[2 * KW + 2 * VW: 2 * KW + 2 * VW + NKV]

    A = b.sb("arenaA", [128, 16384])
    B = b.sb("arenaB", [128, 16384])
    X = b.sb("arenaX", [128, 2048])
    Abf = A[:].bitcast(BF16)
    Bbf = B[:].bitcast(BF16)

    xt = A[:, 0:2048]
    lnt = X[:, 0:2048]
    QC = A
    def lnh_of(abf, fc, t0, t1):
        return abf[:, fc * 2048 + t0: fc * 2048 + t1]
    def lnl_of(abf, fc, t0, t1):
        return abf[:, fc * 2048 + 1024 + t0: fc * 2048 + 1024 + t1]

    pbuf = B[:, 0:2048]
    ph_ = Bbf[:, 4096:8192]
    pl_ = Bbf[:, 6144:8192]
    phv = Bbf[:, 4096:6144]
    kvb0 = Bbf[:, 8192:12288]
    kvb1 = Bbf[:, 12288:16384]
    qh_s = Bbf[:, 16384:17408]
    ql_s = Bbf[:, 17408:18432]
    qh_sl = [Bbf[:, 16384:17408], Bbf[:, 24576:25600]]
    ql_sl = [Bbf[:, 17408:18432], Bbf[:, 25600:26624]]
    inv_rs = B[:, 9216:10240]
    ctmp = B[:, 10240:11264]
    qsq = B[:, 11264:12288]
    x2 = B

    eyet = b.sb("eyet", [128, 128])
    cosqt = b.sb("cosqt", [128, T])
    sinmt = b.sb("sinmt", [128, T])
    onesf = b.sb("onesf", [128, 128])
    onesb = b.sb("onesb", [128, 64], BF16)
    ematt = b.sb("ematt", [16, ER])
    rb1t = b.sb("rb1t", [RHID, 1])
    rbgt = b.sb("rbgt", [NEXP, 1])
    fbias = b.sb("fbias", [1, 512])
    st1 = b.sb("st1", [128, 8])
    sc1 = b.sb("sc1", [128, 32])
    sc2 = b.sb("sc2", [128, 32])
    knorm2 = b.sb("knorm2", [1, NKV])
    kn_all = b.sb("kn_all", [1, NKV * NCORES])
    kmax2 = b.sb("kmax2", [1, NKV])
    qmax2 = b.sb("qmax2", [1, NH])
    mh = b.sb("mh", [128, NH])
    ev = b.sb("ev", [128, 1024])
    kvstage = b.sb("kvstage", [128, 1024], BF16)
    wstage = b.sb("wstage", [128, 3072])
    wbh = b.sb("wbh", [128, 3072], BF16)
    wbl = b.sb("wbl", [128, 3072], BF16)
    h_fm = b.sb("h_fm", [RHID, T])
    h_h = b.sb("h_h", [RHID, T], BF16)
    h_l = b.sb("h_l", [RHID, T], BF16)
    r2h = b.sb("r2h", [RHID, 32], BF16)
    r2l = b.sb("r2l", [RHID, 32], BF16)
    lg_fm = b.sb("lg_fm", [16, T])
    lg_tm = b.sb("lg_tm", [128, 16])
    msk1 = b.sb("msk1", [128, 16])
    msk2 = b.sb("msk2", [128, 16])
    ind1 = b.sb("ind1", [128, 16])
    cf_tm = b.sb("cf_tm", [128, 16])
    cf_fm = b.sb("cf_fm", [16, T])
    wxh = b.sb("wxh", [128, 2 * T], BF16)
    wxl = b.sb("wxl", [128, 2 * T], BF16)
    outt = b.sb("outt", [128, 1024])
    cfe = X[:, 0:1024]
    xa_t = X[:, 1024:2048]

    psS0 = b.ps("psS0", [128, 1024])
    psS1 = b.ps("psS1", [128, 1024])
    psC = b.ps("psC", [128, 1024])
    psR = b.ps("psR", [128, 512])
    psX = b.ps("psX", [128, 512])
    bank = [psS0[:, 0:512], psS0[:, 512:1024], psS1[:, 0:512], psS1[:, 512:1024],
            psC[:, 0:512], psC[:, 512:1024], psR[:, :], psX[:, :]]

    for s in ["d", "dw0", "dw1", "dw2", "dkv0", "dkv1", "dout", "dout0", "dout1", "vr",
              "dn", "dfb", "dx0", "dx1", "tt", "tq", "ts", "tp", "va", "vh", "vb", "sa", "se", "sh", "ga", "gl", "cc"]:
        b.sem(s)

    for dst, src in [(eyet[:], eye), (cosqt[:], cosq), (sinmt[:], sinm),
                     (ematt[:], emat), (rb1t[:], rb1), (rbgt[:], rbg)]:
        b.dma("sync", dst, src, "d")
    b.wait("vector", "d")
    b.op("vector", lambda v: v.memset(onesf[:], 1.0))
    b.op("vector", lambda v: v.tensor_copy(onesb[:], onesf[:, 0:64]), sem="va")

    def emit_ln(src_tile, lnt=None, sq_guard=-1):
        if lnt is None:
            lnt = X[:, 0:2048]
        if sq_guard == -1:
            b.wait("scalar", "tt")
        else:
            b.wait_tok("scalar", sq_guard)
        b.op("vector", lambda v, s_=src_tile: v.reduce_sum(st1[:, 0:1], s_, axis=AX.X))
        t_nm = b.op("vector", lambda v: v.tensor_scalar(
            out=st1[:, 1:2], in0=st1[:, 0:1], scalar1=-1.0 / E, scalar2=None,
            op0=OP.mult), sem="va")
        b.wait_tok("scalar", t_nm)
        t_sq = b.op("scalar", lambda s, s_=src_tile, lnt=lnt: s.activation(
            out=lnt, in_=s_, func=AF.Square, bias=st1[:, 1:2], scale=1.0,
            accum_out=st1[:, 2:3]), sem="sa")
        b.wait_tok("vector", t_sq)
        b.op("vector", lambda v: v.tensor_scalar(
            out=st1[:, 3:4], in0=st1[:, 2:3], scalar1=1.0 / E, scalar2=LN_EPS,
            op0=OP.mult, op1=OP.add), sem="va")
        b.wait("scalar", "va")
        t_rt = b.op("scalar", lambda s: s.activation(
            out=st1[:, 4:5], in_=st1[:, 3:4], func=AF.Sqrt), sem="sa")
        b.wait_tok("vector", t_rt)
        b.op("vector", lambda v: v.reciprocal(st1[:, 5:6], st1[:, 4:5]))
        b.op("vector", lambda v: v.tensor_tensor(out=st1[:, 6:7], in0=st1[:, 5:6], in1=st1[:, 5:6], op=OP.mult))
        b.op("vector", lambda v: v.tensor_tensor(out=st1[:, 6:7], in0=st1[:, 6:7], in1=st1[:, 3:4], op=OP.mult))
        b.op("vector", lambda v: v.tensor_scalar(out=st1[:, 6:7], in0=st1[:, 6:7], scalar1=-0.5, scalar2=1.5, op0=OP.mult, op1=OP.add))
        b.op("vector", lambda v: v.tensor_tensor(out=st1[:, 7:8], in0=st1[:, 5:6], in1=st1[:, 6:7], op=OP.mult))
        return b.op("vector", lambda v, s_=src_tile, lnt=lnt: v.tensor_scalar(
            out=lnt, in0=s_, scalar1=st1[:, 1:2], scalar2=st1[:, 7:8],
            op0=OP.add, op1=OP.mult), sem="va")

    def emit_transpose_splits(tc, dst_bf, apply_tok, lnt=None):
        if lnt is None:
            lnt = X[:, 0:2048]
        toks = [None] * FCH
        t_tr = None
        for fc in range(FCH):
            slot = fc % 2
            pst = psS0[:, slot * 512: slot * 512 + 128]
            b.wait_tok("tensor", apply_tok)
            if fc >= 2:
                b.wait_tok("tensor", toks[fc - 2])
            t_tr = b.op("tensor", lambda t, fc=fc, pst=pst, lnt=lnt: t.transpose(
                pst, lnt[:, fc * 128:(fc + 1) * 128], eyet[:]), sem="tt")
            hs = lnh_of(dst_bf, fc, tc * 128, tc * 128 + 128)
            ls = lnl_of(dst_bf, fc, tc * 128, tc * 128 + 128)
            b.wait_tok("scalar", t_tr)
            t_hi = b.op("scalar", lambda s, pst=pst, hs=hs: s.activation(
                out=hs, in_=pst, func=AF.Copy), sem="sa", nd=True)
            b.wait_tok("vector", t_hi)
            toks[fc] = b.op("vector", lambda v, pst=pst, hs=hs, ls=ls: v.tensor_tensor(
                out=ls, in0=pst, in1=hs, op=OP.subtract), sem="vb", nd=True)
        return toks[-1], t_tr

    ap_hist = [None, None]
    tt_hist = [None, None]
    for tc in range(TCH):
        sl_ = tc % 2
        xt_t = A[:, sl_ * 2048:(sl_ + 1) * 2048]
        lnt_t = A[:, 4096 + sl_ * 2048: 4096 + (sl_ + 1) * 2048]
        b.wait_tok("sync", ap_hist[sl_])
        t_x = b.dma("sync", xt_t, x_in.rearrange("(c p) e -> c p e", p=128)[tc], "dx%d" % sl_)
        b.wait_tok("vector", t_x)
        b.wait_tok("scalar", t_x)
        t_ap = emit_ln(xt_t, lnt=lnt_t, sq_guard=tt_hist[sl_])
        ap_hist[sl_] = t_ap
        _, tt_hist[sl_] = emit_transpose_splits(tc, Bbf, t_ap, lnt=lnt_t)

    b.dump("ln1b", B[:])

    if stage < 1:
        t_z = b.op("vector", lambda v: v.memset(ev[:, 0:512], 0.0), sem="va")
        b.wait_tok("sync", t_z)
        for tc in range(TCH):
            for oh in range(4):
                b.dma("sync", y_out[tc * 128:(tc + 1) * 128, oh * 512:(oh + 1) * 512],
                      ev[:, 0:512], "dout")
        b.wait("sync", "dout")
        _replay(b)
        return b

    b._kv_tok = None

    wslot = {"tok": [None, None, None], "split": [None, None, None]}

    def weight_half(dram_ap, ncols, slot, free_tok=-1, split_toks=-1, lo_eng="gpsimd"):
        sl = slice(slot * 1024, slot * 1024 + ncols)
        b.wait_tok("sync", wslot["tok"][slot])
        if wslot["split"][slot] is not None:
            b.wait_tok("sync", wslot["split"][slot][0])
            b.wait_tok("sync", wslot["split"][slot][1])
        dst = wstage[:, sl]
        if dram_ap.ndim == 3:
            dst = dst.rearrange("p (f o) -> p f o", f=dram_ap.shape[1])
        t_d = b.dma("sync", dst, dram_ap, "dw%d" % slot)
        b.wait_tok("vector", t_d)
        b.wait_tok("vector", wslot["tok"][slot])
        b.wait_tok("gpsimd", t_d)
        b.wait_tok("gpsimd", wslot["tok"][slot])
        hs = wbh[:, sl]
        ls = wbl[:, sl]
        t_v = b.op("vector", lambda v, sl=sl, hs=hs: v.tensor_copy(hs, wstage[:, sl]), sem="va", nd=True)
        if lo_eng == "gpsimd":
            b.wait_tok("gpsimd", t_v)
            t_g = b.op("gpsimd", lambda g, sl=sl, hs=hs, ls=ls: g.tensor_tensor(
                out=ls, in0=wstage[:, sl], in1=hs, op=OP.subtract), sem="ga", nd=True)
        else:
            t_g = b.op("vector", lambda v, sl=sl, hs=hs, ls=ls: v.tensor_tensor(
                out=ls, in0=wstage[:, sl], in1=hs, op=OP.subtract), sem="va")
        wslot["split"][slot] = (t_v, t_g)
        return hs, ls, t_v, t_g

    def wslot_consumed(slot, tok):
        wslot["tok"][slot] = tok

    def rope_pair(pst, dst, tsl, sem):
        b.op("vector", lambda v, pst=pst, tsl=tsl: v.tensor_tensor(
            out=ev[:, 0:512], in0=pst, in1=cosqt[:, tsl], op=OP.mult))
        b.op("vector", lambda v, pst=pst, dst=dst, tsl=tsl: v.tensor_tensor(
            out=dst[0:64, :], in0=pst[64:128, :], in1=sinmt[0:64, tsl], op=OP.mult))
        b.op("vector", lambda v, pst=pst, dst=dst, tsl=tsl: v.tensor_tensor(
            out=dst[64:128, :], in0=pst[0:64, :], in1=sinmt[64:128, tsl], op=OP.mult))
        return b.op("vector", lambda v, dst=dst: v.tensor_tensor(
            out=dst, in0=dst, in1=ev[:, 0:512], op=OP.add), sem=sem)

    qkv_state = {"ps": [None, None, None, None]}
    qkv_banks = [psS1[:, 0:512], psS1[:, 512:1024], psS0[:, 0:512], psS0[:, 512:1024]]

    def qkv(w_ap, n_oc, evac, lo_eng="gpsimd"):
        prev_ps_tok = qkv_state["ps"]
        nhalf = 2 * n_oc
        fetched = {}

        def fetch(i):
            if i >= nhalf or i in fetched:
                return
            oc, hf = divmod(i, 2)
            fetched[i] = (weight_half(
                w_ap[oc, hf].rearrange("f p o -> p f o"), 1024, i % 3,
                lo_eng=lo_eng), i % 3)

        for i in range(3):
            fetch(i)
        for oc in range(n_oc):
            halves = [fetched[2 * oc], fetched[2 * oc + 1]]
            for th in range(2):
                pslot = (oc * 2 + th) % 4
                pst = qkv_banks[pslot]
                b.wait_tok("tensor", prev_ps_tok[pslot])
                t_mm = None
                for fc in range(FCH):
                    hf, fcl = divmod(fc, 8)
                    (hs, ls, t_v, t_g), slot = halves[hf]
                    if fcl == 0:
                        b.wait_tok("tensor", t_v)
                        b.wait_tok("tensor", t_g)
                    wh_ = hs[:, fcl * 128:(fcl + 1) * 128]
                    wl_ = ls[:, fcl * 128:(fcl + 1) * 128]
                    rh_ = lnh_of(Bbf, fc, th * 512, th * 512 + 512)
                    rl_ = lnl_of(Bbf, fc, th * 512, th * 512 + 512)
                    t_mm = b.mm3(pst, wh_, wl_, rh_, rl_, start=(fc == 0),
                                 stop=(fc == FCH - 1), sem="tq" if fc == FCH - 1 else None)
                prev_ps_tok[pslot] = evac(oc, th, pst, t_mm)
            wslot_consumed(fetched[2 * oc][1], t_mm)
            wslot_consumed(fetched[2 * oc + 1][1], t_mm)
            fetch(2 * oc + 3)
            fetch(2 * oc + 4)

    def evac_q(oc, th, pst, t_mm):
        tsl = slice(th * 512, (th + 1) * 512)
        dst = QC[:, oc * 1024 + th * 512: oc * 1024 + (th + 1) * 512]
        b.wait_tok("vector", t_mm)
        return rope_pair(pst, dst, tsl, "va")

    def evac_k(oc, th, pst, t_mm):
        j = oc
        tsl = slice(th * 512, (th + 1) * 512)
        f_ = ev[:, 512:1024]
        b.wait_tok("vector", t_mm)
        t_r = rope_pair(pst, f_, tsl, "va")
        kh = kvstage[:, 0:512]
        kl = kvstage[:, 512:1024]
        b.wait_tok("scalar", t_r)
        b.wait_tok("scalar", b._kv_tok)
        t_h = b.op("scalar", lambda s, kh=kh, f_=f_: s.activation(out=kh, in_=f_, func=AF.Copy), sem="sa")
        b.wait_tok("vector", t_h)
        t_l = b.op("vector", lambda v, kl=kl, f_=f_, kh=kh: v.tensor_tensor(
            out=kl, in0=f_, in1=kh, op=OP.subtract), sem="va")
        t_s = b.op("vector", lambda v, f_=f_: v.tensor_tensor(
            out=ev[:, 0:512], in0=f_, in1=f_, op=OP.mult), sem="va")
        b.wait_tok("tensor", t_s)
        t_n = b.op("tensor", lambda t: t.matmul(psX[0:1, :], onesf[:, 0:1], ev[:, 0:512], start=True, stop=True), sem="tt")
        b.wait_tok("vector", t_n)
        b.op("vector", lambda v: v.reduce_max(sc1[0:1, 0:1], psX[0:1, :], axis=AX.X))
        if th == 0:
            b.op("vector", lambda v, j=j: v.tensor_copy(knorm2[:, j:j + 1], sc1[0:1, 0:1]), sem="va")
        else:
            b.op("vector", lambda v, j=j: v.tensor_tensor(
                out=knorm2[:, j:j + 1], in0=knorm2[:, j:j + 1], in1=sc1[0:1, 0:1], op=OP.max), sem="va")
        b.wait_tok("sync", t_l)
        w0 = th * 256
        b.dma("sync", kpay[0, j, :, w0:w0 + 256], kh.bitcast(F32), "d")
        tok = b.dma("sync", kpay[1, j, :, w0:w0 + 256], kl.bitcast(F32), "d")
        b._kv_tok = tok
        return t_r

    def evac_v(oc, th, pst, t_mm):
        j = oc
        f_ = ev[:, 0:512]
        b.wait_tok("vector", t_mm)
        t_cp = b.op("vector", lambda v, f_=f_, pst=pst: v.tensor_copy(f_, pst), sem="va")
        tok = None
        prev = [None, None]
        for sub in range(4):
            tc = th * 4 + sub
            ptr = psC[:, (sub % 2) * 512:(sub % 2) * 512 + 128]
            b.wait_tok("tensor", t_cp)
            b.wait_tok("tensor", prev[sub % 2])
            t_tr = b.op("tensor", lambda t, sub=sub, ptr=ptr, f_=f_: t.transpose(
                ptr, f_[:, sub * 128:(sub + 1) * 128], eyet[:]), sem="tt")
            vh = kvstage[:, 0:128]
            vl = kvstage[:, 128:256]
            b.wait_tok("scalar", t_tr)
            b.wait_tok("scalar", b._kv_tok)
            t_h = b.op("scalar", lambda s, ptr=ptr, vh=vh: s.activation(out=vh, in_=ptr, func=AF.Copy), sem="sa")
            b.wait_tok("vector", t_h)
            t_l = b.op("vector", lambda v, ptr=ptr, vh=vh, vl=vl: v.tensor_tensor(
                out=vl, in0=ptr, in1=vh, op=OP.subtract), sem="va")
            prev[sub % 2] = t_l
            b.wait_tok("sync", t_l)
            b.dma("sync", vpay[0, j, tc], vh.bitcast(F32), "d")
            tok = b.dma("sync", vpay[1, j, tc], vl.bitcast(F32), "d")
            b._kv_tok = tok
        return t_cp

    b.wait("tensor", "vb")
    qkv(wk, NKV, evac_k)
    qkv(wv, NKV, evac_v)

    b.wait("sync", "va")
    b.dma("sync", npay.rearrange("(a j) -> a j", a=1), knorm2[:], "d")
    b.dump("pay", pay_in)

    b.wait("gpsimd", "d")
    rec = b.sems["cc"]; rec[1] += 1
    b.on("gpsimd", lambda g, h=rec[0]: g.collective_compute(
        "AllGather", OP.bypass, replica_groups=[list(range(NCORES))],
        ins=[pay_in.opt()], outs=[pay_out.opt()]).then_inc(h, 1))

    qkv(wq, NH, evac_q, lo_eng="vector")
    b.dump("qc", QC[:])

    if stage >= 2:
        for h in range(NH):
            qsl = QC[:, h * 1024:(h + 1) * 1024]
            t_q = b.op("vector", lambda v, qsl=qsl: v.tensor_tensor(
                out=qsq[:], in0=qsl, in1=qsl, op=OP.mult), sem="va")
            b.wait_tok("tensor", t_q)
            b.op("tensor", lambda t: t.matmul(psR[0:1, :], onesf[:, 0:1], qsq[:, 0:512], start=True, stop=True))
            t_n = b.op("tensor", lambda t: t.matmul(psX[0:1, :], onesf[:, 0:1], qsq[:, 512:1024], start=True, stop=True), sem="tt")
            b.wait_tok("vector", t_n)
            b.op("vector", lambda v: v.reduce_max(sc1[0:1, 0:1], psR[0:1, :], axis=AX.X))
            b.op("vector", lambda v: v.reduce_max(sc1[0:1, 1:2], psX[0:1, :], axis=AX.X))
            b.op("vector", lambda v: v.tensor_tensor(out=sc1[0:1, 0:1], in0=sc1[0:1, 0:1], in1=sc1[0:1, 1:2], op=OP.max))
            b.op("vector", lambda v, h=h: v.tensor_copy(qmax2[:, h:h + 1], sc1[0:1, 0:1]), sem="va")

    b.wait("gpsimd", "cc")
    t_gg = b.op("gpsimd", lambda g: g.memset(sc2[0:1, 0:1], 0.0), sem="ga")
    for e_ in ("sync", "tensor", "vector", "scalar"):
        b.wait_tok(e_, t_gg)
    b.dump("gat", pay_out)

    if stage >= 2:
        t_kn = b.dma("sync", kn_all[:].rearrange("a (r j) -> a r j", r=NCORES),
                     pay_out[:, 2 * KW + 2 * VW: 2 * KW + 2 * VW + NKV], "dn")
        b.wait_tok("vector", t_kn)
        b.op("vector", lambda v: v.reduce_max(
            kmax2[:], kn_all[:].rearrange("a (r j) -> a j r", r=NCORES), axis=AX.X), sem="va")
        for h in range(NH):
            b.op("vector", lambda v, h=h: v.tensor_tensor(
                out=sc2[0:1, h:h + 1], in0=qmax2[:, h:h + 1],
                in1=kmax2[:, h // 2:h // 2 + 1], op=OP.mult),
                sem="va" if h == NH - 1 else None)
        b.wait("scalar", "va")
        t_sq2 = b.op("scalar", lambda s: s.activation(out=sc1[0:1, 0:NH], in_=sc2[0:1, 0:NH], func=AF.Sqrt), sem="sa")
        b.wait_tok("vector", t_sq2)
        t_ng = b.op("vector", lambda v: v.tensor_scalar(
            out=sc1[0:1, NH:2 * NH], in0=sc1[0:1, 0:NH], scalar1=-INV_SQRT_E,
            scalar2=None, op0=OP.mult), sem="va")
        b.wait_tok("tensor", t_ng)
        t_bc = b.op("tensor", lambda t: t.matmul(psR[:, 0:NH], onesf[0:1, :], sc1[0:1, NH:2 * NH], start=True, stop=True), sem="tt")
        b.wait_tok("vector", t_bc)
        b.op("vector", lambda v: v.tensor_copy(mh[:], psR[:, 0:NH]), sem="va")
        b.wait("scalar", "va")

        kvbuf = [kvb0, kvb1]
        flat = pay_out.rearrange("r n -> (r n)")

        def kv_dma(j, blk, tok_free):
            r = blk
            buf = kvbuf[blk % 2]
            b.wait_tok("sync", tok_free)
            base_kh = r * PAY + j * 128 * 512
            base_kl = r * PAY + KW + j * 128 * 512
            base_vh = r * PAY + 2 * KW + j * TCH * 128 * 64
            base_vl = r * PAY + 2 * KW + VW + j * TCH * 128 * 64
            ds = "dkv%d" % (blk % 2)
            b.dma("sync", buf[:, 0:1024].bitcast(F32),
                  flat[base_kh: base_kh + 128 * 512].rearrange("(p w) -> p w", p=128), ds)
            b.dma("sync", buf[:, 1024:2048].bitcast(F32),
                  flat[base_kl: base_kl + 128 * 512].rearrange("(p w) -> p w", p=128), ds)
            b.dma("sync", buf[:, 2048:3072].bitcast(F32).rearrange("p (c w) -> p c w", c=TCH),
                  flat[base_vh: base_vh + TCH * 128 * 64].rearrange("(c p w) -> p c w", c=TCH, p=128), ds)
            return b.dma("sync", buf[:, 3072:4096].bitcast(F32).rearrange("p (c w) -> p c w", c=TCH),
                         flat[base_vl: base_vl + TCH * 128 * 64].rearrange("(c p w) -> p c w", c=TCH, p=128), ds)

        NU = 2 * KCH
        psSslot = [psS0[:, 0:512], psS0[:, 512:1024], psS1[:, 0:512], psS1[:, 512:1024]]
        prev_sweep_pv = ("tq", b.sems["tq"][1])
        init_tok = ("tq", b.sems["tq"][1])
        sweep_last_scores = [init_tok, init_tok]
        ctx_ch_tok = [None]

        def emit_qsplit(hs_, par, guard):
            qsrc = QC[:, hs_ * 1024:(hs_ + 1) * 1024]
            b.wait_tok("vector", guard)
            t_qh = b.op("vector", lambda v, qsrc=qsrc, par=par: v.tensor_copy(
                qh_sl[par], qsrc), sem="vh", nd=True)
            b.wait_tok("gpsimd", t_qh)
            t_ql = b.op("gpsimd", lambda g, qsrc=qsrc, par=par: g.tensor_tensor(
                out=ql_sl[par], in0=qsrc, in1=qh_sl[par], op=OP.subtract), sem="gl", nd=True)
            return t_qh, t_ql

        pending_split = emit_qsplit(0, 0, init_tok)
        for j in range(NKV):
            for hh in range(2):
                h = 2 * j + hh
                par = h % 2
                qh_s = qh_sl[par]
                ql_s = ql_sl[par]
                b.wait("tensor", "se")
                b.wait("tensor", "sh", b.sems["sh"][1])
                b.wait("scalar", "vh")
                b.wait("scalar", "vr", b.sems["vr"][1])
                b.wait("scalar", "tp", b.sems["tp"][1])
                b.wait("vector", "tp")
                b.wait("vector", "sh", b.sems["sh"][1])
                b.wait_tok("vector", ctx_ch_tok[0])
                t_z0 = b.op("vector", lambda v: v.memset(ctmp[:], 0.0), sem="vr")
                acc_tok = [t_z0, t_z0]
                t_qh, t_ql = pending_split
                b.wait_tok("tensor", t_qh)
                b.wait_tok("tensor", t_ql)
                e_tok = [None] * NU
                h_tok = [None] * NU
                l_tok = [None] * NU
                s_tok = [None] * NU
                p_tok = [None] * NU
                a_tok = [None] * NU
                d_tok = [None] * 8
                d_tok[0] = kv_dma(j, 0, prev_sweep_pv)
                d_tok[1] = kv_dma(j, 1, prev_sweep_pv)

                def kv_tiles(u):
                    c, qh2 = divmod(u, 2)
                    blk, lc = divmod(c, 8)
                    buf = kvbuf[blk % 2]
                    return (blk, qh2,
                            buf[:, 0:1024][:, lc * 128:(lc + 1) * 128],
                            buf[:, 1024:2048][:, lc * 128:(lc + 1) * 128],
                            buf[:, 2048:3072][:, lc * 128:(lc + 1) * 128],
                            buf[:, 3072:4096][:, lc * 128:(lc + 1) * 128])

                def emit_scores(u):
                    blk, qh2, Kh, Kl, Vh, Vl = kv_tiles(u)
                    b.wait_tok("tensor", d_tok[blk])
                    if u >= 4:
                        b.wait_tok("tensor", e_tok[u - 4])
                    s_tok[u] = b.mm3(psSslot[u % 4], Kh, Kl,
                                     qh_s[:, qh2 * 512:(qh2 + 1) * 512],
                                     ql_s[:, qh2 * 512:(qh2 + 1) * 512],
                                     start=True, stop=True, sem="ts")

                for u0 in range(3):
                    emit_scores(u0)
                for u in range(NU):
                    c, qh2 = divmod(u, 2)
                    blk, _, Kh, Kl, Vh, Vl = kv_tiles(u)
                    sl = u % 4
                    pb = pbuf[:, sl * 512:(sl + 1) * 512]
                    phb = phv[:, sl * 512:(sl + 1) * 512]
                    plb = pl_[:, sl * 512:(sl + 1) * 512]
                    b.wait_tok("scalar", s_tok[u])
                    if u >= 4:
                        b.wait_tok("scalar", l_tok[u - 4])
                        b.wait_tok("scalar", a_tok[u - 4])
                    e_tok[u] = b.op("scalar", lambda s, ps_=psSslot[sl], pb=pb, h=h: s.activation(
                        out=pb, in_=ps_, func=AF.Exp, bias=mh[:, h:h + 1],
                        scale=INV_SQRT_E), sem="se", nd=True)
                    b.wait_tok("scalar", e_tok[u])
                    if u >= 4:
                        b.wait_tok("scalar", p_tok[u - 4])
                        b.wait_tok("scalar", l_tok[u - 4])
                    h_tok[u] = b.op("scalar", lambda s, pb=pb, phb=phb: s.activation(
                        out=phb, in_=pb, func=AF.Copy), sem="sh", nd=True)
                    b.wait_tok("vector", h_tok[u])
                    if u >= 4:
                        b.wait_tok("vector", p_tok[u - 4])
                    l_tok[u] = b.op("vector", lambda v, pb=pb, phb=phb, plb=plb: v.tensor_tensor(
                        out=plb, in0=pb, in1=phb, op=OP.subtract), sem="vh", nd=True)
                    b.wait_tok("vector", acc_tok[qh2])
                    acc_tok[qh2] = b.op("vector", lambda v, pb=pb, qh2=qh2: v.tensor_tensor(
                        out=ctmp[:, qh2 * 512:(qh2 + 1) * 512],
                        in0=ctmp[:, qh2 * 512:(qh2 + 1) * 512], in1=pb, op=OP.add),
                        sem="vr", nd=True)
                    a_tok[u] = acc_tok[qh2]
                    b.wait_tok("tensor", h_tok[u])
                    b.wait_tok("tensor", l_tok[u])
                    first, last = (c == 0), (c == KCH - 1)
                    p_tok[u] = b.mm3(psC[:, qh2 * 512:(qh2 + 1) * 512], Vh, Vl, phb, plb,
                                     start=first, stop=last, sem="tp")
                    if qh2 == 1 and c % 8 == 7 and blk + 2 < 8:
                        d_tok[blk + 2] = kv_dma(j, blk + 2, p_tok[u])
                    if u + 3 < NU:
                        emit_scores(u + 3)
                prev_sweep_pv = p_tok[NU - 1]
                sweep_last_scores[par] = s_tok[NU - 1]
                if h + 1 < NH:
                    pending_split = emit_qsplit(h + 1, (h + 1) % 2,
                                                sweep_last_scores[(h + 1) % 2])
                b.wait_tok("tensor", acc_tok[0])
                b.wait_tok("tensor", acc_tok[1])
                b.op("tensor", lambda t: t.matmul(psR[0:1, :], onesf[:, 0:1], ctmp[:, 0:512], start=True, stop=True))
                t_rsum = b.op("tensor", lambda t: t.matmul(psX[0:1, :], onesf[:, 0:1], ctmp[:, 512:1024], start=True, stop=True), sem="tt")
                b.wait_tok("vector", t_rsum)
                b.wait_tok("vector", p_tok[NU - 1])
                b.op("vector", lambda v: v.reciprocal(ev[0:1, 0:512], psR[0:1, :]))
                t_rc = b.op("vector", lambda v: v.reciprocal(ev[0:1, 512:1024], psX[0:1, :]), sem="va")
                b.wait_tok("tensor", t_rc)
                t_b0 = b.op("tensor", lambda t: t.matmul(psX[:, :], onesf[0:1, :], ev[0:1, 0:512], start=True, stop=True), sem="tt")
                b.wait_tok("vector", t_b0)
                t_c0 = b.op("vector", lambda v: v.tensor_copy(inv_rs[:, 0:512], psX[:]), sem="va")
                b.wait_tok("tensor", t_c0)
                t_b1 = b.op("tensor", lambda t: t.matmul(psX[:, :], onesf[0:1, :], ev[0:1, 512:1024], start=True, stop=True), sem="tt")
                b.wait_tok("vector", t_b1)
                b.op("vector", lambda v: v.tensor_copy(inv_rs[:, 512:1024], psX[:]), sem="va")
                t_nm = b.op("vector", lambda v: v.tensor_tensor(out=ctmp[:], in0=psC[:], in1=inv_rs[:], op=OP.mult), sem="va")
                ch = Abf[:, h * 2048: h * 2048 + 1024]
                cl = Abf[:, h * 2048 + 1024: h * 2048 + 2048]
                b.wait_tok("scalar", t_nm)
                t_ch = b.op("scalar", lambda s, ch=ch: s.activation(out=ch, in_=ctmp[:], func=AF.Copy), sem="sa", nd=True)
                ctx_ch_tok[0] = t_ch
                b.wait_tok("vector", t_ch)
                b.op("vector", lambda v, ch=ch, cl=cl: v.tensor_tensor(
                    out=cl, in0=ctmp[:], in1=ch, op=OP.subtract), sem="vb", nd=True)

        b.dump("ctx", A[:])

    if stage >= 3:
        prev_ev_tok = [None] * TCH
        prev_x_tok = None
        wo_f = {}

        def wo_fetch(t):
            if t >= 64 or t in wo_f:
                return
            wo_f[t] = (weight_half(wo[t // FCH, t % FCH], 512, t % 3), t % 3)

        wo_fetch(0); wo_fetch(1); wo_fetch(2)
        for oh in range(4):
            for fc in range(FCH):
                tno = oh * FCH + fc
                (hs, ls, t_v, t_g), slot = wo_f[tno]
                b.wait_tok("tensor", t_v)
                b.wait_tok("tensor", t_g)
                t_mm = None
                for qc in range(TCH):
                    ch = Abf[:, fc * 2048 + qc * 128: fc * 2048 + qc * 128 + 128]
                    cl = Abf[:, fc * 2048 + 1024 + qc * 128: fc * 2048 + 1024 + qc * 128 + 128]
                    if fc == 0:
                        b.wait_tok("tensor", prev_ev_tok[qc])
                    t_mm = b.mm3(bank[qc], ch, cl, hs[:, 0:512], ls[:, 0:512],
                                 start=(fc == 0), stop=(fc == FCH - 1),
                                 sem="tq" if qc == TCH - 1 else None)
                wslot_consumed(slot, t_mm)
                wo_fetch(tno + 3)
            for qc in range(TCH):
                b.wait_tok("sync", prev_x_tok)
                t_x = b.dma("sync", ev[:, 0:512],
                            x_in[qc * 128:(qc + 1) * 128, oh * 512:(oh + 1) * 512], "d")
                b.wait_tok("vector", t_x)
                b.wait("vector", "tq")
                prev_ev_tok[qc] = b.op("vector", lambda v, qc=qc, oh=oh: v.tensor_tensor(
                    out=x2[:, qc * 2048 + oh * 512: qc * 2048 + (oh + 1) * 512],
                    in0=bank[qc], in1=ev[:, 0:512], op=OP.add), sem="vb", nd=True)
                prev_x_tok = prev_ev_tok[qc]
        b.dump("x2", B[:])

    if stage >= 4:
        for tc in range(TCH):
            src = x2[:, tc * 2048:(tc + 1) * 2048]
            b.wait("vector", "vb")
            b.wait("scalar", "vb", b.sems["vb"][1])
            t_ap = emit_ln(src)
            emit_transpose_splits(tc, Abf, t_ap)

        hs1, ls1, t_v, t_g = weight_half(
            rw1.rearrange("f p r -> p f r"), FCH * RHID, 0)
        b.wait("tensor", "vb")
        b.wait_tok("tensor", t_v)
        b.wait_tok("tensor", t_g)
        for th in range(2):
            pst = psR if th == 0 else psX
            t_mm = None
            for fc in range(FCH):
                wh_ = hs1[:, fc * RHID:(fc + 1) * RHID]
                wl_ = ls1[:, fc * RHID:(fc + 1) * RHID]
                rh_ = lnh_of(Abf, fc, th * 512, th * 512 + 512)
                rl_ = lnl_of(Abf, fc, th * 512, th * 512 + 512)
                t_mm = b.mm3(pst[0:RHID, :], wh_, wl_, rh_, rl_, start=(fc == 0),
                             stop=(fc == FCH - 1), sem="tq" if fc == FCH - 1 else None)
            wslot_consumed(0, t_mm)
            b.wait_tok("scalar", t_mm)
            b.op("scalar", lambda s, pst=pst, th=th: s.activation(
                out=h_fm[:, th * 512:(th + 1) * 512], in_=pst[0:RHID, :],
                func=AF.Silu, bias=rb1t[:], scale=1.0), sem="sa")
        b.wait("vector", "sa")
        t_hh = b.op("vector", lambda v: v.tensor_copy(h_h[:], h_fm[:]), sem="va")
        b.wait_tok("gpsimd", t_hh)
        t_hl = b.op("gpsimd", lambda g: g.tensor_tensor(out=h_l[:], in0=h_fm[:], in1=h_h[:], op=OP.subtract), sem="ga")
        b.wait_tok("sync", wslot["tok"][1])
        if wslot["split"][1] is not None:
            b.wait_tok("sync", wslot["split"][1][0])
            b.wait_tok("sync", wslot["split"][1][1])
        t_w2 = b.dma("sync", wstage[0:RHID, 1024:1056], rw2, "dw1")
        b.wait_tok("vector", t_w2)
        t_r2h = b.op("vector", lambda v: v.tensor_copy(r2h[:], wstage[0:RHID, 1024:1056]), sem="va")
        b.wait_tok("gpsimd", t_r2h)
        t_r2l = b.op("gpsimd", lambda g: g.tensor_tensor(out=r2l[:], in0=wstage[0:RHID, 1024:1056], in1=r2h[:], op=OP.subtract), sem="ga")
        wslot["split"][1] = (t_r2h, t_r2l)
        for tk in (t_hh, t_r2h):
            b.wait_tok("tensor", tk)
        for tk in (t_hl, t_r2l):
            b.wait_tok("tensor", tk)
        for th in range(2):
            pst = psR if th == 0 else psX
            t_mm = b.mm3(pst[0:32, :], r2h[:], r2l[:],
                         h_h[:, th * 512:(th + 1) * 512],
                         h_l[:, th * 512:(th + 1) * 512],
                         start=True, stop=True, sem="tq")
            b.wait_tok("vector", t_mm)
            b.op("vector", lambda v, pst=pst, th=th: v.tensor_scalar(
                out=lg_fm[:, th * 512:(th + 1) * 512], in0=pst[0:16, :],
                scalar1=rbgt[:], scalar2=None, op0=OP.add), sem="va")
        b.wait("tensor", "va")
        prev_cf = None
        for tc in range(TCH):
            t_tr = b.op("tensor", lambda t, tc=tc: t.transpose(
                psR[:, 0:16], lg_fm[:, tc * 128:(tc + 1) * 128], eyet[0:16, 0:16]), sem="tt")
            b.wait_tok("vector", t_tr)
            b.wait_tok("vector", prev_cf)
            b.op("vector", lambda v: v.tensor_copy(lg_tm[:], psR[:, 0:16]))
            b.op("vector", lambda v: v.reduce_max(sc1[:, 0:1], lg_tm[:], axis=AX.X))
            b.op("vector", lambda v: v.tensor_scalar(out=ind1[:], in0=lg_tm[:],
                 scalar1=sc1[:, 0:1], scalar2=None, op0=OP.is_equal))
            b.op("vector", lambda v: v.scalar_tensor_tensor(out=msk1[:], in0=ind1[:],
                 scalar=-1e30, in1=lg_tm[:], op0=OP.mult, op1=OP.add))
            b.op("vector", lambda v: v.reduce_max(sc1[:, 1:2], msk1[:], axis=AX.X))
            b.op("vector", lambda v: v.tensor_scalar(out=msk2[:], in0=msk1[:],
                 scalar1=sc1[:, 1:2], scalar2=None, op0=OP.is_equal))
            t_d1 = b.op("vector", lambda v: v.tensor_tensor(out=sc1[:, 2:3], in0=sc1[:, 1:2], in1=sc1[:, 0:1], op=OP.subtract), sem="va")
            b.wait_tok("scalar", t_d1)
            t_z = b.op("scalar", lambda s: s.activation(out=sc1[:, 3:4], in_=sc1[:, 2:3], func=AF.Exp), sem="sa")
            b.wait_tok("vector", t_z)
            b.op("vector", lambda v: v.tensor_scalar(out=sc1[:, 4:5], in0=sc1[:, 3:4], scalar1=1.0, scalar2=None, op0=OP.add))
            b.op("vector", lambda v: v.reciprocal(sc1[:, 5:6], sc1[:, 4:5]))
            b.op("vector", lambda v: v.tensor_tensor(out=sc1[:, 6:7], in0=sc1[:, 3:4], in1=sc1[:, 5:6], op=OP.mult))
            b.op("vector", lambda v: v.tensor_scalar(out=cf_tm[:], in0=ind1[:], scalar1=sc1[:, 5:6], scalar2=None, op0=OP.mult))
            t_cf = b.op("vector", lambda v: v.scalar_tensor_tensor(
                out=cf_tm[:], in0=msk2[:], scalar=sc1[:, 6:7], in1=cf_tm[:],
                op0=OP.mult, op1=OP.add), sem="va")
            b.wait_tok("tensor", t_cf)
            t_tr2 = b.op("tensor", lambda t: t.transpose(psX[0:16, 0:128], cf_tm[:], eyet[:]), sem="tt")
            b.wait_tok("vector", t_tr2)
            prev_cf = b.op("vector", lambda v, tc=tc: v.tensor_copy(
                cf_fm[:, tc * 128:(tc + 1) * 128], psX[0:16, 0:128]), sem="va")
        b.dump("cf", cf_fm[:])

        b.wait_tok("tensor", prev_cf)
        for ec in range(2):
            for th in range(2):
                pst = psR if th == 0 else psX
                t_mm = b.op("tensor", lambda t, ec=ec, th=th, pst=pst: t.matmul(
                    pst[:, :], ematt[:, ec * 128:(ec + 1) * 128],
                    cf_fm[:, th * 512:(th + 1) * 512], start=True, stop=True), sem="tt")
                b.wait_tok("vector", t_mm)
                b.op("vector", lambda v, th=th, pst=pst: v.tensor_copy(
                    cfe[:, th * 512:(th + 1) * 512], pst[:, :]), sem="va")
            halves = []
            for hf in range(2):
                slot = hf
                hs, ls, t_v, t_g = weight_half(
                    wa[ec, hf].rearrange("f p o -> p f o"), 1024, slot)
                halves.append((hs, ls, t_v, t_g))
            for th in range(2):
                pst = psS0[:, th * 512:(th + 1) * 512] if ec == 0 else psS1[:, th * 512:(th + 1) * 512]
                t_mm = None
                for fc in range(FCH):
                    hf, fcl = divmod(fc, 8)
                    hs, ls, t_v, t_g = halves[hf]
                    if fcl == 0 and th == 0:
                        b.wait_tok("tensor", t_v)
                        b.wait_tok("tensor", t_g)
                    wh_ = hs[:, fcl * 128:(fcl + 1) * 128]
                    wl_ = ls[:, fcl * 128:(fcl + 1) * 128]
                    rh_ = lnh_of(Abf, fc, th * 512, th * 512 + 512)
                    rl_ = lnl_of(Abf, fc, th * 512, th * 512 + 512)
                    t_mm = b.mm3(pst, wh_, wl_, rh_, rl_, start=(fc == 0),
                                 stop=(fc == FCH - 1), sem="tq" if fc == FCH - 1 else None)
                wslot_consumed(0, t_mm)
                wslot_consumed(1, t_mm)
                b.wait_tok("vector", t_mm)
                b.wait("vector", "va")
                t_xa = b.op("vector", lambda v, pst=pst, th=th: v.tensor_tensor(
                    out=xa_t[:, th * 512:(th + 1) * 512], in0=pst,
                    in1=cfe[:, th * 512:(th + 1) * 512], op=OP.mult), sem="va")
                b.wait_tok("scalar", t_xa)
                t_sh = b.op("scalar", lambda s, ec=ec, th=th: s.activation(
                    out=wxh[:, ec * 1024 + th * 512: ec * 1024 + (th + 1) * 512],
                    in_=xa_t[:, th * 512:(th + 1) * 512], func=AF.Copy), sem="sa")
                b.wait_tok("vector", t_sh)
                b.op("vector", lambda v, ec=ec, th=th: v.tensor_tensor(
                    out=wxl[:, ec * 1024 + th * 512: ec * 1024 + (th + 1) * 512],
                    in0=xa_t[:, th * 512:(th + 1) * 512],
                    in1=wxh[:, ec * 1024 + th * 512: ec * 1024 + (th + 1) * 512],
                    op=OP.subtract), sem="vb")

        b.wait("tensor", "vb")
        prev_ev_tok = [None] * TCH
        prev_out_tok = [None, None]
        ff_f = {}

        def ff_fetch(t):
            if t >= 72 or t in ff_f:
                return
            ohx, u = divmod(t, 18)
            ap = wff[ohx, u] if u < FCH else wb[ohx, u - FCH]
            ff_f[t] = (weight_half(ap, 512, t % 3), t % 3)

        ff_fetch(0); ff_fetch(1); ff_fetch(2)
        for oh in range(4):
            t_fb = b.dma("sync", fbias[:], ffb[oh:oh + 1, :], "dfb")
            for fc in range(FCH):
                tno = oh * 18 + fc
                (hs, ls, t_v, t_g), slot = ff_f[tno]
                b.wait_tok("tensor", t_v)
                b.wait_tok("tensor", t_g)
                t_mm = None
                for qc in range(TCH):
                    lh = lnh_of(Abf, fc, qc * 128, qc * 128 + 128)
                    ll = lnl_of(Abf, fc, qc * 128, qc * 128 + 128)
                    if fc == 0:
                        b.wait_tok("tensor", prev_ev_tok[qc])
                    t_mm = b.mm3(bank[qc], lh, ll, hs[:, 0:512], ls[:, 0:512],
                                 start=(fc == 0), stop=False,
                                 sem="tq" if qc == TCH - 1 else None)
                wslot_consumed(slot, t_mm)
                ff_fetch(tno + 3)
            for ec in range(2):
                tno = oh * 18 + FCH + ec
                (hs, ls, t_v, t_g), slot = ff_f[tno]
                b.wait_tok("tensor", t_v)
                b.wait_tok("tensor", t_g)
                t_mm = None
                for qc in range(TCH):
                    xh_ = wxh[:, ec * 1024 + qc * 128: ec * 1024 + qc * 128 + 128]
                    xl_ = wxl[:, ec * 1024 + qc * 128: ec * 1024 + qc * 128 + 128]
                    t_mm = b.mm3(bank[qc], xh_, xl_, hs[:, 0:512], ls[:, 0:512],
                                 start=False, stop=False,
                                 sem="tq" if qc == TCH - 1 else None)
                wslot_consumed(slot, t_mm)
                ff_fetch(tno + 3)
            b.wait_tok("tensor", t_fb)
            t_last = [None] * TCH
            for qc in range(TCH):
                t_last[qc] = b.op("tensor", lambda t, qc=qc: t.matmul(
                    bank[qc], onesf[0:1, :], fbias[:], start=False, stop=True), sem="tq")
            for qc in range(TCH):
                oslot = qc % 2
                b.wait_tok("vector", t_last[qc])
                b.wait_tok("vector", prev_out_tok[oslot])
                t_ev = b.op("vector", lambda v, qc=qc, oh=oh, oslot=oslot: v.tensor_tensor(
                    out=outt[:, oslot * 512:(oslot + 1) * 512],
                    in0=bank[qc],
                    in1=x2[:, qc * 2048 + oh * 512: qc * 2048 + (oh + 1) * 512],
                    op=OP.add), sem="vb", nd=True)
                prev_ev_tok[qc] = t_ev
                b.wait_tok("sync", t_ev)
                prev_out_tok[oslot] = b.dma(
                    "sync", y_out[qc * 128:(qc + 1) * 128, oh * 512:(oh + 1) * 512],
                    outt[:, oslot * 512:(oslot + 1) * 512], "dout%d" % oslot)

    if stage < 4:
        t_z = b.op("vector", lambda v: v.memset(ev[:, 0:512], 0.0), sem="va")
        b.wait_tok("sync", t_z)
        for tc in range(TCH):
            for oh in range(4):
                b.dma("sync", y_out[tc * 128:(tc + 1) * 128, oh * 512:(oh + 1) * 512],
                      ev[:, 0:512], "dout")

    b.wait("sync", "dout")
    b.wait("sync", "dout0")
    b.wait("sync", "dout1")
    _replay(b)
    return b


def _replay(b):
    blk = b.stack.enter_context(b.nc.Block())
    by_eng = {"sync": [], "tensor": [], "vector": [], "scalar": [], "gpsimd": []}
    for eng, fn in b.ops:
        by_eng[eng].append(fn)

    @blk.sync
    def _(e):
        for fn in by_eng["sync"]:
            fn(e)

    @blk.tensor
    def _(e):
        for fn in by_eng["tensor"]:
            fn(e)

    @blk.vector
    def _(e):
        for fn in by_eng["vector"]:
            fn(e)

    @blk.scalar
    def _(e):
        for fn in by_eng["scalar"]:
            fn(e)

    @blk.gpsimd
    def _(e):
        for fn in by_eng["gpsimd"]:
            fn(e)

    b.stack.close()



def _prep_weights(inp):
    f32 = np.float32
    WQ = np.asarray(inp["L0_attn_W_Q"], f32)
    WK = np.asarray(inp["L0_attn_W_K"], f32)
    WV = np.asarray(inp["L0_attn_W_V"], f32)
    WO = np.asarray(inp["L0_attn_W_O"], f32)
    permq = (np.arange(HD)[None, :] * NH + np.arange(NH)[:, None]).reshape(-1)
    permk = (np.arange(HD)[None, :] * NKV + np.arange(NKV)[:, None]).reshape(-1)

    def lhsT_tiles(Wp, noc):
        Wt = Wp.T
        t = Wt.reshape(2, 8, 128, noc, 128).transpose(3, 0, 1, 2, 4)
        return np.ascontiguousarray(t)

    wq_t = lhsT_tiles(WQ[permq], NH)
    wk_t = lhsT_tiles(WK[permk], NKV)
    wv_t = lhsT_tiles(WV[permk], NKV)
    wo_t = np.ascontiguousarray(WO.T.reshape(FCH, 128, 4, 512).transpose(2, 0, 1, 3))
    wff_t = np.ascontiguousarray(np.asarray(inp["L0_ffn_W"], f32).T
                                 .reshape(FCH, 128, 4, 512).transpose(2, 0, 1, 3))
    Amat = np.asarray(inp["L0_ffn_A"], f32).reshape(ER, E)
    wa_t = np.ascontiguousarray(Amat.T.reshape(2, 8, 128, 2, 128).transpose(3, 0, 1, 2, 4))
    Bmat = np.asarray(inp["L0_ffn_B"], f32).transpose(0, 2, 1).reshape(ER, E)
    wb_t = np.ascontiguousarray(Bmat.reshape(2, 128, 4, 512).transpose(2, 0, 1, 3))
    rw1_t = np.ascontiguousarray(np.asarray(inp["L0_ffn_rw1"], f32).T.reshape(FCH, 128, RHID))
    rw2_t = np.ascontiguousarray(np.asarray(inp["L0_ffn_rw2"], f32).T)
    ffb = np.ascontiguousarray(np.asarray(inp["L0_ffn_bias"], f32).reshape(4, 512))
    rb1 = np.ascontiguousarray(np.asarray(inp["L0_ffn_rb1"], f32).reshape(RHID, 1))
    rbg = np.ascontiguousarray((np.asarray(inp["L0_ffn_rb2"], f32)[:NEXP]
                                + np.asarray(inp["L0_ffn_gates"], f32)).reshape(NEXP, 1))
    emat = np.zeros((NEXP, ER), f32)
    for e_ in range(NEXP):
        emat[e_, e_ * RANK:(e_ + 1) * RANK] = 1.0
    return dict(wq=wq_t, wk=wk_t, wv=wv_t, wo=wo_t, wff=wff_t, wa=wa_t, wb=wb_t,
                rw1=rw1_t, rw2=rw2_t, ffb=ffb, rb1=rb1, rbg=rbg,
                eye=np.eye(128, dtype=f32), emat=emat)


def _rope_tables(core):
    d = np.arange(HD)
    invfreq = 10000.0 ** (-(d % 64).astype(np.float64) / 64.0)
    pos = np.arange(core * T, (core + 1) * T, dtype=np.float64)
    ang = invfreq[:, None] * pos[None, :]
    cos = np.cos(ang).astype(np.float32)
    sin = np.sin(ang).astype(np.float32)
    sinm = sin.copy()
    sinm[:64] = -sin[:64]
    return np.ascontiguousarray(cos), np.ascontiguousarray(sinm)


def make_in_maps(inputs):
    x = np.asarray(inputs["L0_attn_x"], np.float32).reshape(S, E)
    w = _prep_weights(inputs)
    in_maps = []
    for c in range(NCORES):
        cos, sinm = _rope_tables(c)
        m = dict(w)
        m["x"] = np.ascontiguousarray(x[c * T:(c + 1) * T])
        m["cosq"] = cos
        m["sinm"] = sinm
        in_maps.append(m)
    return in_maps


def kernel(**inputs):
    b = build()
    in_maps = make_in_maps(inputs)
    res = run_bass_kernel_spmd(b.nc, in_maps, core_ids=list(range(NCORES)))
    out = np.concatenate([r["y"] for r in res.results], axis=0)
    return out.reshape(1, S, E)
